# version 10
# baseline (speedup 1.0000x reference)
import numpy as np
import concourse.bass as bass
import concourse.mybir as mybir
from concourse.bass_utils import run_bass_kernel_spmd

F32 = mybir.dt.float32
F32R = mybir.dt.float32r
AF = mybir.ActivationFunctionType
OP = mybir.AluOpType

D = 1024
G = 8
NCORE = 8
RING = 4
TC = 256
MMDT = 'f32'

E10 = float(np.exp(np.float64(10.0)))
LOG2_128 = float(128 * np.log(np.float64(2.0)))
TINY = 1e-30

MSS = 8
MSL = TC // MSS
MSK = 32
MSJ = MSK + MSL
RTPW = MSK + TC
MSA0 = G * RTPW
MSB0 = MSA0 + G * MSS
SPKB0 = MSB0 + G * MSS

ENGS = ('sync', 'gpsimd', 'tensor', 'vector', 'scalar')


def build_nc(T):
    NCH = T // TC
    nc = bass.Bass("TRN2", target_bir_lowering=False, detect_race_conditions=False)

    embT = nc.dram_tensor("embT", [D, T], F32, kind="ExternalInput")
    wK = nc.dram_tensor("wK", [G, 128, 1024], F32, kind="ExternalInput")
    wV = nc.dram_tensor("wV", [G, 128, 1024], F32, kind="ExternalInput")
    wTd = nc.dram_tensor("wTd", [G, 128, 1024], F32, kind="ExternalInput")
    wR = nc.dram_tensor("wR", [G, 128, 1024], F32, kind="ExternalInput")
    wF = nc.dram_tensor("wF", [G, 128, 1024], F32, kind="ExternalInput")
    aT = nc.dram_tensor("aT", [16, 1024], F32, kind="ExternalInput")
    bm = nc.dram_tensor("bm", [16, 1024], F32, kind="ExternalInput")
    svec = nc.dram_tensor("svec", [3, 1024], F32, kind="ExternalInput")
    scr = nc.dram_tensor("scr", [128], F32, kind="Internal")
    wEff = nc.dram_tensor("wEff", [G, 128, 1024], F32, kind="Internal")
    outT = nc.dram_tensor("outT", [D, T], F32, kind="ExternalOutput")

    def dram_ap(h, off, dims):
        return bass.AP(h, off, [list(d) for d in dims])

    ctx = []

    def sb(name, shape, dtype=F32):
        t = nc.sbuf_tensor(name, shape, dtype)
        h = t.__enter__()
        ctx.append(t)
        return h

    def psum(name):
        t = nc.psum_tensor(name, [128, 512], F32)
        h = t.__enter__()
        ctx.append(t)
        return h

    wring = sb("wring", [128, RING * 1024])
    wq = sb("wq", [128, G * 1024])
    emb2 = [sb("embA", [128, G, TC + 1]), sb("embB", [128, G, TC + 1])]
    X3 = [sb("XA", [128, G, TC]), sb("XB", [128, G, TC]),
          sb("XC", [128, G, TC])]
    EK = sb("EK", [128, G, TC])
    EKV = sb("EKV", [128, G, TC])
    GEK = sb("GEK", [128, G, TC])
    VV = sb("VVb", [128, G, TC])
    TD = sb("TD", [128, G, TC])
    RT = sb("RT", [128, G, TC])
    Ab = sb("Ab", [128, G, 2, TC])
    Bb = sb("Bb", [128, G, TC])
    SPK = sb("SPK", [128, G, TC])
    FB = sb("FB", [128, G, TC])
    aTs = sb("aTs", [16, 1024])
    bs = sb("bs", [16, 1024])
    ones = sb("ones", [128, TC])
    st_mean = sb("st_mean", [1, 2 * TC])
    st_m2 = sb("st_m2", [1, 2 * TC])
    eps5 = sb("eps5", [1, 1])
    flv = sb("flv", [128, 8])
    fls = sb("fls", [128, 8])
    flg = sb("flg", [128, 8])
    smalls = {}
    for n in ('mixs', 'mx1m', 'gds', 'biass', 'disc', 't75', 'fsc', 'Lg',
              'epsf', 'thr', 'acar', 'bcar', 'fcar',
              'rr', 'lb', 'rexp', 'tin', 'bclip'):
        smalls[n] = sb(n, [128, G])
    S = type('S', (), smalls)
    wmax = sb("wmax", [128, 1])
    tmax = sb("tmax", [128, 1])
    scalec = sb("scalec", [128, 1])
    invc = sb("invc", [128, 1])
    trow = sb("trow", [1, 128])
    scal = sb("scal", [1, 1])
    pb = [psum(f"pb{i}") for i in range(8)]

    sems = {}
    for n in ('E', 'W', 'P', 'Q', 'M', 'SS', 'VN', 'O'):
        t = nc.semaphore(n)
        sems[n] = t.__enter__()
        ctx.append(t)

    ops = {e: [] for e in ENGS}
    cnt = {n: 0 for n in sems}
    marks = {}

    flushfn = {
        'vector': lambda e: e.tensor_copy(flv[:, 0:4], flv[:, 4:8]),
        'scalar': lambda e: e.activation(fls[:, 0:4], fls[:, 4:8], AF.Identity),
        'gpsimd': lambda e: e.tensor_tensor(flg[:, 0:4], flg[:, 4:8],
                                            flg[:, 4:8], OP.add),
    }

    def emit(eng, fn, waits=(), incs=(), flush=False):
        if waits and eng == 'vector':
            ops[eng].append((flushfn[eng], tuple(waits), ()))
            ops[eng].append((flushfn[eng], (), ()))
            waits = ()
        if flush and incs and eng in flushfn:
            ops[eng].append((fn, tuple(waits), ()))
            if eng == 'scalar':
                ops[eng].append((flushfn[eng], (), ()))
            ops[eng].append((flushfn[eng], (), tuple(incs)))
        else:
            ops[eng].append((fn, tuple(waits), tuple(incs)))
        for s, a in incs:
            cnt[s] += a

    def mark(key, sname):
        marks[key] = (sname, cnt[sname])

    def wt(key):
        return marks[key]

    def mm_dtype(ap):
        if MMDT == 'f32r':
            return ap.bitcast(F32R)
        return ap

    emit('gpsimd', lambda e: e.memset(ones[:, :], 1.0))
    emit('gpsimd', lambda e: e.memset(S.fsc[:, :], 1.0))
    emit('gpsimd', lambda e: e.memset(S.Lg[:, :], 0.0))
    for n in ('acar', 'bcar', 'fcar'):
        emit('gpsimd', lambda e, n=n: e.memset(smalls[n][:, :], 0.0))
    emit('gpsimd', lambda e: e.memset(emb2[0][:, :, 0:1], 0.0))
    emit('gpsimd', lambda e: e.memset(eps5[:, :], 1e-5))
    emit('gpsimd', lambda e: e.memset(flv[:, :], 0.0))
    emit('gpsimd', lambda e: e.memset(fls[:, :], 0.0))
    emit('gpsimd', lambda e: e.memset(flg[:, :], 0.0))
    emit('gpsimd', lambda e: e.memset(S.t75[:, :], float(2.0 ** -80)),
         incs=[('Q', 1)], flush=True)
    mark('msinit', 'Q')
    emit('vector', lambda e: e.tensor_mul(S.t75[:, :], S.t75[:, :], S.t75[:, :]),
         waits=[wt('msinit')])
    emit('vector', lambda e: e.tensor_single_scalar(
        S.disc[:, :], S.t75[:, :], 0.0, OP.is_equal))
    emit('vector', lambda e: e.tensor_scalar(
        S.epsf[:, :], S.fsc[:, :], 1e-6, TINY, OP.mult, OP.add))
    emit('vector', lambda e: e.tensor_scalar(
        S.tin[:, :], S.fsc[:, :], float(2.0 ** 64), float(2.0 ** 63),
        OP.mult, OP.min))
    emit('vector', flushfn['vector'])
    emit('vector', lambda e: e.tensor_scalar(
        S.thr[:, :], S.tin[:, :], float(2.0 ** 64), None, OP.mult),
        incs=[('VN', 1)], flush=True)
    mark('vinit', 'VN')

    for r, dst in ((0, S.mixs), (1, S.gds), (2, S.biass)):
        emit('gpsimd', (lambda r=r, dst=dst: lambda e: e.dma_start(
            out=dst[:, :], in_=dram_ap(svec, r * 1024, [[1, 128], [128, G]]),
            allow_slow_non_contiguous=True))(),
            incs=[('P', 16)])
    emit('gpsimd', lambda e: e.dma_start(out=aTs[:, :], in_=aT.ap()), incs=[('P', 16)])
    emit('gpsimd', lambda e: e.dma_start(out=bs[:, :], in_=bm.ap()), incs=[('P', 16)])
    mark('misc_dma', 'P')

    emit('scalar', lambda e: e.activation(S.gds[:, :], S.gds[:, :], AF.Exp),
         waits=[wt('misc_dma'), wt('vinit')], incs=[('SS', 1)], flush=True)
    mark('prolog_sc', 'SS')

    emit('vector', lambda e: e.tensor_scalar(
        S.mx1m[:, :], S.mixs[:, :], -1.0, 1.0, OP.mult, OP.add),
        waits=[wt('misc_dma')])

    for eb in range(G):
        waits = [wt(('amax', eb - RING))] if eb >= RING else []
        emit('sync', (lambda eb=eb: lambda e: e.dma_start(
            out=wring[:, (eb % RING) * 1024:(eb % RING + 1) * 1024],
            in_=wF.ap()[eb]))(), waits=waits, incs=[('W', 16)])
        mark(('wdma', eb), 'W')
        emit('vector', (lambda eb=eb: lambda e: e.tensor_reduce(
            tmax[:, :], wring[:, (eb % RING) * 1024:(eb % RING + 1) * 1024],
            mybir.AxisListType.X, OP.max, apply_absolute_value=True))(),
            waits=[wt(('wdma', eb))])
        emit('vector', flushfn['vector'])
        emit('vector', flushfn['vector'])
        if eb == 0:
            emit('vector', lambda e: e.tensor_copy(wmax[:, :], tmax[:, :]),
                 incs=[('VN', 1)], flush=True)
        else:
            emit('vector', lambda e: e.tensor_max(wmax[:, :], wmax[:, :], tmax[:, :]),
                 incs=[('VN', 1)], flush=True)
        mark(('amax', eb), 'VN')
    mark('absmax_done', 'VN')

    emit('gpsimd', lambda e: e.dma_start(
        out=dram_ap(scr, 0, [[1, 128]]), in_=wmax[:, :]),
        waits=[wt('absmax_done')], incs=[('P', 16)])
    mark('scr_out', 'P')
    emit('gpsimd', lambda e: e.dma_start(
        out=trow[:, :], in_=dram_ap(scr, 0, [[128, 1], [1, 128]])),
        waits=[wt('scr_out')], incs=[('P', 16)])
    mark('scr_in', 'P')
    emit('vector', lambda e: e.tensor_reduce(
        scal[:, :], trow[:, :], mybir.AxisListType.X, OP.max),
        waits=[wt('scr_in')], incs=[('VN', 1)], flush=True)
    mark('amax_all', 'VN')
    emit('tensor', lambda e: e.matmul(pb[4][:, 0:1], ones[0:1, 0:128],
                                      scal[:, :], start=True, stop=True),
         waits=[wt('amax_all')], incs=[('M', 1)])
    mark('bc_amax', 'M')
    emit('vector', lambda e: e.tensor_scalar(
        scalec[:, :], pb[4][:, 0:1], 1.0 / 7.0, None, OP.mult),
        waits=[wt('bc_amax')])
    for _ in range(6):
        emit('vector', flushfn['vector'])
    emit('vector', lambda e: e.reciprocal(invc[:, :], scalec[:, :]))
    emit('vector', flushfn['vector'])
    emit('vector', flushfn['vector'])
    emit('vector', lambda e: e.reciprocal(invc[:, :], scalec[:, :]))
    emit('vector', flushfn['vector'])
    emit('vector', flushfn['vector'])
    emit('vector', lambda e: e.reciprocal(invc[:, :], scalec[:, :]),
         incs=[('VN', 1)], flush=True)
    emit('vector', flushfn['vector'])
    mark('scale_ready', 'VN')

    def flat(buf, n=1024):
        return bass.AP(buf, 0, [[G * TC, 128], [1, n]])

    WTMP, QTMP = EK, TD
    for eb in range(G):
        w2 = G + eb
        waits = [wt(('weff', eb - RING))] if eb >= RING else [wt(('amax', G - 1))]
        emit('sync', (lambda eb=eb: lambda e: e.dma_start(
            out=wring[:, (eb % RING) * 1024:(eb % RING + 1) * 1024],
            in_=wF.ap()[eb]))(), waits=waits, incs=[('W', 16)])
        mark(('wdma', w2), 'W')
        slot = eb % RING
        wsl = wring[:, slot * 1024:(slot + 1) * 1024]
        emit('vector', (lambda wsl=wsl: lambda e: e.tensor_scalar(
            flat(WTMP), wsl, invc[:, 0:1], None, OP.mult))(),
            waits=[wt('scale_ready'), wt(('wdma', w2))])
        qwaits = [wt(('wq', eb - 1))] if eb > 0 else []
        emit('gpsimd', lambda e: e.memset(flat(QTMP), -8.0),
             waits=qwaits, incs=[('Q', 1)], flush=True)
        mark(('qms', eb), 'Q')
        for j in range(-8, 8):
            emit('vector', (lambda j=j: lambda e: e.scalar_tensor_tensor(
                flat(QTMP), flat(WTMP), j + 0.5, flat(QTMP), OP.is_gt, OP.add))(),
                waits=([wt(('qms', eb))] if j == -8 else []))
            emit('vector', flushfn['vector'])
        emit('vector', lambda e: e.tensor_single_scalar(
            flat(QTMP), flat(QTMP), 7.0, OP.min))
        emit('vector', flushfn['vector'])
        emit('vector', flushfn['vector'])
        emit('vector', (lambda eb=eb: lambda e: e.tensor_scalar(
            wq[:, eb * 1024:(eb + 1) * 1024], flat(QTMP), scalec[:, 0:1],
            None, OP.mult))(), incs=[('VN', 1)], flush=True)
        emit('vector', flushfn['vector'])
        mark(('wq', eb), 'VN')
        for kt in range(G):
            bank = pb[6] if kt < 4 else pb[7]
            col = (kt % 4) * 128
            emit('tensor', (lambda eb=eb, kt=kt, bank=bank, col=col: lambda e: e.matmul(
                bank[:, col:col + 128], aTs[:, kt * 128:(kt + 1) * 128],
                bs[:, eb * 128:(eb + 1) * 128], start=True, stop=True))(),
                waits=([wt('misc_dma'), wt(('wq', eb))] if kt == 0 else []),
                incs=([('M', 1)] if kt == G - 1 else []))
        mark(('ab', eb), 'M')
        for kt in range(G):
            bank = pb[6] if kt < 4 else pb[7]
            col = (kt % 4) * 128
            lo = eb * 1024 + kt * 128
            emit('vector', (lambda lo=lo, bank=bank, col=col: lambda e: e.tensor_add(
                wq[:, lo:lo + 128], wq[:, lo:lo + 128], bank[:, col:col + 128]))(),
                waits=([wt(('ab', eb))] if kt == 0 else []),
                incs=([('VN', 1)] if kt == G - 1 else []), flush=True)
        mark(('weff', eb), 'VN')
        emit('gpsimd', (lambda eb=eb: lambda e: e.dma_start(
            out=wEff.ap()[eb], in_=wq[:, eb * 1024:(eb + 1) * 1024]))(),
            waits=[wt(('weff', eb))], incs=[('P', 16)])
        mark(('weffd', eb), 'P')

    WROW = G * 1024

    def rtp_ap(j):
        return bass.AP(wq, j, [[WROW, 128], [RTPW, G], [MSL, MSS]])

    def rtp_xin(g):
        return wq[:, g * RTPW + MSK: g * RTPW + MSK + TC]

    def spk_ap(j):
        return bass.AP(wq, SPKB0 + j, [[WROW, 128], [MSS * MSJ, G], [MSJ, MSS]])

    def spk_out(g):
        return bass.AP(wq, SPKB0 + g * MSS * MSJ + MSK,
                       [[WROW, 128], [MSJ, MSS], [1, MSL]])

    STF0 = SPKB0 + G * MSS * MSJ
    st_rstd = wq[0:1, STF0:STF0 + 2 * TC]
    st_scr = wq[0:1, STF0 + 2 * TC:STF0 + 4 * TC]

    msA = wq[:, MSA0:MSA0 + G * MSS]
    msB = wq[:, MSB0:MSB0 + G * MSS]

    for g in range(G):
        emit('vector', (lambda g=g: lambda e: e.tensor_single_scalar(
            wq[:, g * RTPW: g * RTPW + MSK], ones[:, 0:MSK], 0.0, OP.mult))(),
            waits=([wt(('weffd', G - 1))] if g == 0 else []))

    PROJS = (('td', wTd, TD, AF.Exp, 1.0), ('v', wV, VV, AF.Copy, 1.0),
             ('k', wK, EK, AF.Exp, 1.0), ('r', wR, RT, AF.Exp, -1.0))
    wmarks = [None] * (2 * G)
    evseq = []
    state = {'widx': 2 * G}

    def wdma(src_ap, extra=()):
        i = state['widx']; state['widx'] += 1
        slot = i % RING
        dwaits = list(extra)
        prev = i - RING
        if prev >= 2 * G:
            dwaits.append(wt(wmarks[prev]))
        elif prev >= G:
            dwaits.append(wt(('weff', prev - G)))
        emit('sync', (lambda src_ap=src_ap, slot=slot: lambda e: e.dma_start(
            out=wring[:, slot * 1024:(slot + 1) * 1024], in_=src_ap))(),
            waits=dwaits, incs=[('W', 16)])
        mark(('wdma', i), 'W')
        return i, slot

    def proj_group(cc, p, eb, gate):
        pn, wh, dbuf, fn, sc = PROJS[p]
        i, slot = wdma(wh.ap()[eb])
        gi = len(evseq)
        bank = pb[gi % 4]
        Xc = X3[cc % 3]
        key = ('mmg', 'p', cc, p, eb)
        for k in range(G):
            waits = []
            if k == 0:
                waits.append(wt(('wdma', i)))
                waits.append(wt(('mix', cc)))
                if gi >= 3:
                    waits.append(wt(evseq[gi - 3]))
            emit('tensor', (lambda bank=bank, slot=slot, k=k, Xc=Xc: lambda e: e.matmul(
                bank[:, 0:TC],
                mm_dtype(wring[:, slot * 1024 + k * 128: slot * 1024 + (k + 1) * 128]),
                mm_dtype(Xc[:, k, :]), start=(k == 0), stop=(k == G - 1)))(),
                waits=waits, incs=([('M', 1)] if k == G - 1 else []))
        mark(key, 'M')
        wmarks.append(key)
        ewaits = [wt(key)]
        if gate is not None and eb == 0:
            ewaits.append(wt(gate))
        emit('scalar', (lambda bank=bank, dbuf=dbuf, fn=fn, sc=sc, eb=eb: lambda e:
            e.activation(dbuf[:, eb, :], bank[:, 0:TC], fn, scale=sc))(),
            waits=ewaits, incs=[('SS', 1)], flush=(eb == G - 1))
        ekey = ('evac', cc, p, eb)
        mark(ekey, 'SS')
        evseq.append(ekey)

    def ffn_group(m, eb):
        extra = [wt(('weffd', eb))] if m == 0 else []
        i, slot = wdma(wEff.ap()[eb], extra)
        gi = len(evseq)
        bank = pb[gi % 4]
        key = ('mmg', 'f', m, eb)
        for k in range(G):
            waits = [wt(('x1ng', m, k))]
            if k == 0:
                waits.append(wt(('wdma', i)))
                if gi >= 3:
                    waits.append(wt(evseq[gi - 3]))
            emit('tensor', (lambda bank=bank, slot=slot, k=k: lambda e: e.matmul(
                bank[:, 0:2 * TC],
                mm_dtype(wring[:, slot * 1024 + k * 128: slot * 1024 + (k + 1) * 128]),
                mm_dtype(Ab[:, k, :, :]), start=(k == 0), stop=(k == G - 1)))(),
                waits=waits, incs=([('M', 1)] if k == G - 1 else []))
        mark(key, 'M')
        wmarks.append(key)
        return key, bank

    def emit_embdma(cc):
        EMBb = emb2[cc % 2]
        waits = [wt(('x1', cc - 2))] if cc >= 2 else []
        if cc == 0:
            src = dram_ap(embT, 0, [[T, 128], [T * 128, G], [1, TC]])
            dst = EMBb[:, :, 1:TC + 1]
        else:
            src = dram_ap(embT, cc * TC - 1, [[T, 128], [T * 128, G], [1, TC + 1]])
            dst = EMBb[:, :, :]
        emit('sync', (lambda src=src, dst=dst: lambda e: e.dma_start(out=dst, in_=src))(),
             waits=waits, incs=[('E', 16)])
        mark(('embdma', cc), 'E')

    def emit_mix(cc):
        EMBb = emb2[cc % 2]
        Xc = X3[cc % 3]
        for g in range(G):
            waits = []
            if g == 0:
                waits.append(wt(('embdma', cc)))
                if cc == 0:
                    waits.append(wt('msinit'))
                if cc >= 3:
                    waits.append(wt(('sum1', 1, (cc - 3) // 2)))
                    waits.append(wt(('sq1', (cc - 3) // 2)))
            emit('vector', (lambda g=g, EMBb=EMBb, Xc=Xc: lambda e: e.tensor_scalar(
                Xc[:, g, :], EMBb[:, g, 1:TC + 1], S.mixs[:, g:g + 1], None, OP.mult))(),
                waits=waits)
            emit('vector', (lambda g=g, EMBb=EMBb, Xc=Xc: lambda e: e.scalar_tensor_tensor(
                Xc[:, g, :], EMBb[:, g, 0:TC], S.mx1m[:, g:g + 1], Xc[:, g, :],
                OP.mult, OP.add))(),
                incs=([('VN', 1)] if g == G - 1 else []), flush=True)
        mark(('mix', cc), 'VN')

    def ln_stats(tag, cc):
        W2 = 2 * TC
        emit('vector', lambda e: e.tensor_scalar(
            st_mean[:, :], pb[4][0:1, 0:W2], 1.0 / D, None, OP.mult),
            waits=[wt(('sum1', tag, cc))])
        emit('vector', flushfn['vector'])
        emit('vector', lambda e: e.tensor_scalar(
            st_m2[:, :], pb[5][0:1, 0:W2], 1.0 / D, None, OP.mult),
            waits=[wt(('sum2', tag, cc))])
        emit('vector', flushfn['vector'])
        emit('vector', lambda e: e.tensor_scalar(
            st_mean[:, :], pb[4][0:1, 0:W2], 1.0 / D, None, OP.mult))
        emit('vector', flushfn['vector'])
        emit('vector', lambda e: e.tensor_scalar(
            st_m2[:, :], pb[5][0:1, 0:W2], 1.0 / D, None, OP.mult))
        emit('vector', flushfn['vector'])
        emit('vector', lambda e: e.tensor_mul(st_scr, st_mean[:, :], st_mean[:, :]))
        emit('vector', flushfn['vector'])
        emit('vector', lambda e: e.tensor_sub(st_rstd, st_m2[:, :], st_scr),
             incs=[('VN', 1)], flush=True)
        mark(('var', tag, cc), 'VN')
        emit('scalar', flushfn['scalar'], waits=[wt(('var', tag, cc))])
        emit('scalar', flushfn['scalar'])
        emit('scalar', lambda e: e.activation(
            st_scr, st_rstd, AF.Ln, bias=eps5[:, :]))
        emit('scalar', flushfn['scalar'])
        emit('scalar', lambda e: e.activation(
            st_rstd, st_scr, AF.Exp, scale=-0.5),
            incs=[('SS', 1)], flush=True)
        mark(('rstd', tag, cc), 'SS')
        emit('vector', flushfn['vector'], waits=[wt(('rstd', tag, cc))])
        emit('vector', lambda e: e.tensor_mul(st_m2[:, :], st_mean[:, :], st_rstd),
             incs=[('VN', 1)], flush=True)
        mark(('m2', tag, cc), 'VN')
        emit('tensor', lambda e: e.matmul(
            pb[6][:, 0:W2], ones[0:1, 0:128], st_rstd, start=True, stop=True),
            waits=[wt(('m2', tag, cc))], incs=[('M', 1)])
        emit('tensor', lambda e: e.matmul(
            pb[7][:, 0:W2], ones[0:1, 0:128], st_m2[:, :], start=True, stop=True),
            incs=[('M', 1)])
        mark(('bcast', tag, cc), 'M')

    emit_embdma(0)
    emit_embdma(1)
    emit_mix(0)
    for p in range(4):
        for eb in range(G):
            proj_group(0, p, eb, None)

    for c in range(NCH):
        EMBc = emb2[c % 2]
        Xc = X3[c % 3]
        nx = c + 1
        last = (c == NCH - 1)

        if not last:
            emit_mix(nx)

        for g in range(G):
            emit('vector', (lambda g=g: lambda e: e.tensor_scalar(
                EK[:, g, :], EK[:, g, :], E10, S.fsc[:, g:g + 1], OP.min, OP.mult))(),
                waits=([wt(('evac', c, 2, G - 1))] if g == 0 else []))
        for g in range(G):
            waits = []
            if g == 0:
                waits.append(wt(('evac', c, 1, G - 1)))
                if c % 2 == 0 and c >= 2:
                    waits.append(wt(('sum1', 2, c // 2 - 1)))
                    waits.append(wt(('sq2', c // 2 - 1)))
                elif c % 2 == 1:
                    waits.append(wt(('rseed2', c - 1)))
            emit('vector', (lambda g=g: lambda e: e.tensor_mul(
                EKV[:, g, :], EK[:, g, :], VV[:, g, :]))(), waits=waits)
        for g in range(G):
            waits = []
            if g == 0:
                if c == 0:
                    waits.append(wt('prolog_sc'))
                elif c % 2 == 0:
                    waits.append(wt(('sum1', 2, c // 2 - 1)))
                    waits.append(wt(('sq2', c // 2 - 1)))
                else:
                    waits.append(wt(('rseed2', c - 1)))
            emit('vector', (lambda g=g: lambda e: e.tensor_scalar(
                GEK[:, g, :], EK[:, g, :], S.gds[:, g:g + 1], None, OP.mult))(),
                waits=waits)
        for g in range(G):
            waits = []
            if g == 0:
                waits.append(wt(('evac', c, 0, G - 1)))
                if c >= 2:
                    waits.append(wt(('mmg', 'f', c // 2 - 1, G - 1)))
            emit('vector', (lambda g=g, par=c % 2: lambda e: e.tensor_tensor_scan(
                Ab[:, g, par, :], TD[:, g, :], EKV[:, g, :], S.acar[:, g:g + 1],
                OP.mult, OP.add))(), waits=waits)
        for g in range(G):
            emit('vector', (lambda g=g: lambda e: e.tensor_tensor_scan(
                Bb[:, g, :], TD[:, g, :], EK[:, g, :], S.bcar[:, g:g + 1],
                OP.mult, OP.add))(),
                incs=([('VN', 1)] if g == G - 1 else []), flush=(g == G - 1))
        mark(('scdone', c), 'VN')
        for g in range(G):
            fwaits = []
            if g == 0 and c % 2 == 0 and c >= 2:
                fwaits.append(wt(('outdma2', c // 2 - 1)))
            emit('vector', (lambda g=g: lambda e: e.tensor_scalar(
                FB[:, g, :], Bb[:, g, :], S.thr[:, g:g + 1], S.disc[:, g:g + 1],
                OP.is_ge, OP.mult))(), waits=fwaits)
        for g in range(G):
            emit('vector', (lambda g=g: lambda e: e.tensor_tensor_scan(
                FB[:, g, :], ones[:, :], FB[:, g, :], S.fcar[:, g:g + 1],
                OP.mult, OP.max))())
        emit('vector', flushfn['vector'])
        emit('vector', flushfn['vector'])
        emit('vector', lambda e: e.tensor_copy(S.fcar[:, :], FB[:, :, TC - 1]))
        emit('vector', flushfn['vector'])

        if not last:
            for eb in range(G):
                proj_group(nx, 0, eb, ('scdone', c))

        emit('vector', lambda e: e.tensor_mul(
            VV[:, :, :], GEK[:, :, :], VV[:, :, :]))
        emit('vector', (lambda par=c % 2: lambda e: e.tensor_add(
            EK[:, :, :], VV[:, :, :], Ab[:, :, par, :]))(),
            incs=[('VN', 1)], flush=True)
        mark(('numdone', c), 'VN')
        for g in range(G):
            emit('vector', (lambda g=g: lambda e: e.scalar_tensor_tensor(
                GEK[:, g, :], Bb[:, g, :], S.epsf[:, g:g + 1], GEK[:, g, :],
                OP.add, OP.add))(),
                incs=([('VN', 1)] if g == G - 1 else []), flush=(g == G - 1))
        mark(('dendone', c), 'VN')
        if not last:
            emit('vector', lambda e: e.tensor_scalar(
                S.bclip[:, :], Bb[:, :, TC - 1], 1e-35, 1e18, OP.max, OP.min),
                incs=[('VN', 1)], flush=True)
            mark(('bclip', c), 'VN')

        if not last:
            for eb in range(G):
                proj_group(nx, 1, eb, ('numdone', c))
            emit('scalar', lambda e: e.activation(S.lb[:, :], S.bclip[:, :], AF.Ln),
                 waits=[wt(('bclip', c))], incs=[('SS', 1)], flush=True)
            mark(('lnb', c), 'SS')

        emit('vector', lambda e: e.tensor_scalar(
            EKV[:, :, :], RT[:, :, :], 1.0, None, OP.add),
            waits=[wt(('evac', c, 3, G - 1))], incs=[('VN', 1)], flush=True)
        mark(('sigdone', c), 'VN')
        emit('vector', lambda e: e.tensor_scalar(
            FB[:, :, :], FB[:, :, :], -1.0, 1.0, OP.mult, OP.add))
        if c >= 1:
            emit('vector', lambda e: e.tensor_copy(
                bass.AP(wq, 0, [[WROW, 128], [RTPW, G], [1, MSK]]),
                bass.AP(wq, TC, [[WROW, 128], [RTPW, G], [1, MSK]])))

        swaits = [wt(('dendone', c))]
        if c % 2 == 0 and c >= 2:
            swaits.append(wt(('outdma1', c // 2 - 1)))
        emit('scalar', lambda e: e.activation(SPK[:, :, :], GEK[:, :, :], AF.Ln),
             waits=swaits)
        emit('scalar', (lambda Xc=Xc: lambda e: e.activation(
            Xc[:, :, :], SPK[:, :, :], AF.Exp, scale=-1.0))(),
            waits=[wt(('mmg', 'p', c, 3, G - 1))], incs=[('SS', 1)], flush=True)
        mark(('rseed1', c), 'SS')
        emit('scalar', lambda e: e.activation(SPK[:, :, :], EKV[:, :, :], AF.Ln),
             waits=[wt(('sigdone', c))])
        emit('scalar', lambda e: e.activation(
            RT[:, :, :], SPK[:, :, :], AF.Exp, scale=-1.0),
            incs=[('SS', 1)], flush=True)
        mark(('rseed2', c), 'SS')

        emit('vector', (lambda Xc=Xc: lambda e: e.tensor_mul(
            SPK[:, :, :], GEK[:, :, :], Xc[:, :, :]))(),
            waits=[wt(('rseed2', c))])
        emit('vector', lambda e: e.tensor_scalar(
            SPK[:, :, :], SPK[:, :, :], -1.0, 2.0, OP.mult, OP.add))
        emit('vector', (lambda Xc=Xc: lambda e: e.tensor_mul(
            Xc[:, :, :], SPK[:, :, :], Xc[:, :, :]))())
        emit('vector', (lambda Xc=Xc: lambda e: e.tensor_mul(
            Xc[:, :, :], EK[:, :, :], Xc[:, :, :]))(),
            incs=[('VN', 1)], flush=True)
        mark(('wkvdone', c), 'VN')

        if not last:
            for eb in range(G):
                proj_group(nx, 2, eb, ('wkvdone', c))

        emit('vector', lambda e: e.tensor_mul(
            EKV[:, :, :], EKV[:, :, :], RT[:, :, :]))
        emit('vector', lambda e: e.tensor_scalar(
            EKV[:, :, :], EKV[:, :, :], -1.0, 2.0, OP.mult, OP.add))
        emit('vector', lambda e: e.tensor_mul(
            SPK[:, :, :], EKV[:, :, :], RT[:, :, :]),
            incs=[('VN', 1)], flush=True)
        mark(('rnewt', c), 'VN')
        emit('vector', (lambda Xc=Xc: lambda e: e.tensor_mul(
            SPK[:, :, :], SPK[:, :, :], Xc[:, :, :]))())
        if not last:
            emit('vector', lambda e: e.tensor_scalar(
                S.rexp[:, :], S.lb[:, :], -1.0, None, OP.mult),
                waits=[wt(('lnb', c))])
            emit('vector', flushfn['vector'])
            emit('vector', lambda e: e.scalar_tensor_tensor(
                S.rexp[:, :], S.Lg[:, :], 42.0, S.rexp[:, :], OP.add, OP.min),
                incs=[('VN', 1)], flush=True)
            mark(('rexp', c), 'VN')
            emit('scalar', lambda e: e.activation(S.rr[:, :], S.rexp[:, :], AF.Exp),
                 waits=[wt(('rexp', c))], incs=[('SS', 1)], flush=True)
            mark(('rr', c), 'SS')

        if not last:
            for eb in range(G):
                proj_group(nx, 3, eb, ('rnewt', c))

        emit('vector', lambda e: e.tensor_mul(
            bass.AP(wq, MSK, [[WROW, 128], [RTPW, G], [1, TC]]),
            SPK[:, :, :], FB[:, :, :]))

        if not last:
            emit('vector', (lambda par=c % 2: lambda e: e.tensor_mul(
                S.acar[:, :], Ab[:, :, par, TC - 1], S.rr[:, :]))(),
                waits=[wt(('rr', c))])
            emit('vector', lambda e: e.tensor_mul(
                S.bcar[:, :], Bb[:, :, TC - 1], S.rr[:, :]))
            emit('vector', lambda e: e.tensor_mul(S.fsc[:, :], S.fsc[:, :], S.rr[:, :]))
            emit('vector', lambda e: e.tensor_sub(S.Lg[:, :], S.Lg[:, :], S.rexp[:, :]))
            emit('vector', lambda e: e.tensor_scalar(
                S.epsf[:, :], S.fsc[:, :], 1e-6, TINY, OP.mult, OP.add))
            emit('vector', lambda e: e.tensor_scalar(
                S.tin[:, :], S.fsc[:, :], float(2.0 ** 64), float(2.0 ** 63),
                OP.mult, OP.min))
            emit('vector', flushfn['vector'])
            emit('vector', lambda e: e.tensor_scalar(
                S.thr[:, :], S.tin[:, :], float(2.0 ** 64), None, OP.mult))

        emit('vector', lambda e: e.tensor_copy(msB[:, :], rtp_ap(0)))
        emit('vector', lambda e: e.tensor_single_scalar(
            spk_ap(0), msB[:, :], 0.2, OP.is_gt))
        for j in range(1, MSJ):
            emit('vector', (lambda j=j: lambda e: e.scalar_tensor_tensor(
                msA[:, :], msB[:, :], 0.3, rtp_ap(j), OP.mult, OP.add))())
            emit('vector', (lambda j=j: lambda e: e.scalar_tensor_tensor(
                msB[:, :], spk_ap(j - 1), -0.2, msA[:, :], OP.mult, OP.add))())
            emit('vector', (lambda j=j: lambda e: e.tensor_single_scalar(
                spk_ap(j), msB[:, :], 0.2, OP.is_gt))())
        emit('vector', flushfn['vector'])
        emit('vector', flushfn['vector'])

        for g in range(G):
            emit('vector', (lambda g=g, EMBc=EMBc, Xc=Xc: lambda e: e.tensor_add(
                Xc[:, g, :], EMBc[:, g, 1:TC + 1], spk_out(g)))(),
                incs=([('VN', 1)] if g == G - 1 else []), flush=True)
        mark(('x1', c), 'VN')

        if c + 2 < NCH:
            emit_embdma(c + 2)

        if c % 2 == 0:
            continue

        m = c // 2
        ce = c - 1
        Xe = X3[ce % 3]
        Xo = X3[c % 3]
        W2 = 2 * TC

        emit('scalar', (lambda Xe=Xe: lambda e: e.activation(
            GEK[:, :, :], Xe[:, :, :], AF.Square))(),
            waits=[wt(('x1', c))])
        emit('scalar', (lambda Xo=Xo: lambda e: e.activation(
            EKV[:, :, :], Xo[:, :, :], AF.Square))(),
            incs=[('SS', 1)], flush=True)
        mark(('sq1', m), 'SS')
        for q, Xq in ((0, Xe), (1, Xo)):
            for g in range(G):
                waits = []
                if q == 0 and g == 0:
                    waits.append(wt(('x1', c)))
                    if m >= 1:
                        waits.append(wt(('stats2read', m - 1)))
                emit('tensor', (lambda g=g, q=q, Xq=Xq: lambda e: e.matmul(
                    pb[4][0:1, q * TC:(q + 1) * TC], ones[:, 0:1], Xq[:, g, :],
                    start=(g == 0), stop=(g == G - 1)))(),
                    waits=waits)
        emit('tensor', lambda e: e.matmul(
            pb[6][0:1, 0:1], ones[:, 0:1], ones[:, 0:1],
            start=True, stop=True), incs=[('M', 1)])
        mark(('sum1', 1, m), 'M')
        for q, SQ in ((0, GEK), (1, EKV)):
            for g in range(G):
                emit('tensor', (lambda g=g, q=q, SQ=SQ: lambda e: e.matmul(
                    pb[5][0:1, q * TC:(q + 1) * TC], ones[:, 0:1], SQ[:, g, :],
                    start=(g == 0), stop=(g == G - 1)))(),
                    waits=([wt(('sq1', m))] if (q == 0 and g == 0) else []))
        emit('tensor', lambda e: e.matmul(
            pb[6][0:1, 4:5], ones[:, 0:1], ones[:, 0:1],
            start=True, stop=True), incs=[('M', 1)])
        mark(('sum2', 1, m), 'M')
        ln_stats(1, m)
        for g in range(G):
            emit('vector', (lambda g=g, Xe=Xe: lambda e: e.tensor_mul(
                Ab[:, g, 0, :], Xe[:, g, :], pb[6][:, 0:TC]))(),
                waits=([wt(('bcast', 1, m))] if g == 0 else []))
            emit('vector', (lambda g=g: lambda e: e.tensor_sub(
                Ab[:, g, 0, :], Ab[:, g, 0, :], pb[7][:, 0:TC]))())
            emit('vector', (lambda g=g, Xo=Xo: lambda e: e.tensor_mul(
                Ab[:, g, 1, :], Xo[:, g, :], pb[6][:, TC:2 * TC]))())
            emit('vector', (lambda g=g: lambda e: e.tensor_sub(
                Ab[:, g, 1, :], Ab[:, g, 1, :], pb[7][:, TC:2 * TC]))(),
                incs=[('VN', 1)], flush=True)
            mark(('x1ng', m, g), 'VN')
        mark(('stats1read', m), 'VN')

        for eb in range(G):
            fkey, fbank = ffn_group(m, eb)
            waits = [wt(fkey)]
            if eb == 0:
                waits.append(wt(('sum2', 1, m)))
            emit('vector', (lambda eb=eb, fbank=fbank: lambda e: e.scalar_tensor_tensor(
                GEK[:, eb, :], fbank[:, 0:TC], S.biass[:, eb:eb + 1],
                Ab[:, eb, 0, :], OP.add, OP.add))(), waits=waits)
            emit('vector', (lambda eb=eb, fbank=fbank: lambda e: e.scalar_tensor_tensor(
                EKV[:, eb, :], fbank[:, TC:2 * TC], S.biass[:, eb:eb + 1],
                Ab[:, eb, 1, :], OP.add, OP.add))(),
                incs=[('VN', 1)], flush=True)
            fx2 = ('fx2', m, eb)
            mark(fx2, 'VN')
            evseq.append(fx2)
        mark(('x2', m), 'VN')

        emit('scalar', lambda e: e.activation(
            SPK[:, :, :], GEK[:, :, :], AF.Square),
            waits=[wt(('x2', m))])
        emit('scalar', lambda e: e.activation(
            FB[:, :, :], EKV[:, :, :], AF.Square),
            incs=[('SS', 1)], flush=True)
        mark(('sq2', m), 'SS')
        for q, XQ in ((0, GEK), (1, EKV)):
            for g in range(G):
                waits = []
                if q == 0 and g == 0:
                    waits.append(wt(('x2', m)))
                    waits.append(wt(('stats1read', m)))
                emit('tensor', (lambda g=g, q=q, XQ=XQ: lambda e: e.matmul(
                    pb[4][0:1, q * TC:(q + 1) * TC], ones[:, 0:1], XQ[:, g, :],
                    start=(g == 0), stop=(g == G - 1)))(),
                    waits=waits)
        emit('tensor', lambda e: e.matmul(
            pb[6][0:1, 8:9], ones[:, 0:1], ones[:, 0:1],
            start=True, stop=True), incs=[('M', 1)])
        mark(('sum1', 2, m), 'M')
        for q, SQ in ((0, SPK), (1, FB)):
            for g in range(G):
                emit('tensor', (lambda g=g, q=q, SQ=SQ: lambda e: e.matmul(
                    pb[5][0:1, q * TC:(q + 1) * TC], ones[:, 0:1], SQ[:, g, :],
                    start=(g == 0), stop=(g == G - 1)))(),
                    waits=([wt(('sq2', m))] if (q == 0 and g == 0) else []))
        emit('tensor', lambda e: e.matmul(
            pb[6][0:1, 12:13], ones[:, 0:1], ones[:, 0:1],
            start=True, stop=True), incs=[('M', 1)])
        mark(('sum2', 2, m), 'M')
        ln_stats(2, m)
        for g in range(G):
            emit('vector', (lambda g=g: lambda e: e.tensor_mul(
                SPK[:, g, :], GEK[:, g, :], pb[6][:, 0:TC]))(),
                waits=([wt(('bcast', 2, m))] if g == 0 else []))
            emit('vector', (lambda g=g: lambda e: e.tensor_sub(
                SPK[:, g, :], SPK[:, g, :], pb[7][:, 0:TC]))(),
                incs=([('VN', 1)] if g == G - 1 else []), flush=True)
        mark(('oute', m), 'VN')
        for g in range(G):
            emit('vector', (lambda g=g: lambda e: e.tensor_mul(
                FB[:, g, :], EKV[:, g, :], pb[6][:, TC:2 * TC]))())
            emit('vector', (lambda g=g: lambda e: e.tensor_sub(
                FB[:, g, :], FB[:, g, :], pb[7][:, TC:2 * TC]))(),
                incs=([('VN', 1)] if g == G - 1 else []), flush=True)
        mark(('outo', m), 'VN')
        mark(('stats2read', m), 'VN')

        emit('gpsimd', (lambda ce=ce: lambda e: e.dma_start(
            out=dram_ap(outT, ce * TC, [[T, 128], [T * 128, G], [1, TC]]),
            in_=SPK[:, :, :]))(),
            waits=[wt(('oute', m))], incs=[('O', 16)])
        mark(('outdma1', m), 'O')
        emit('gpsimd', (lambda c=c: lambda e: e.dma_start(
            out=dram_ap(outT, c * TC, [[T, 128], [T * 128, G], [1, TC]]),
            in_=FB[:, :, :]))(),
            waits=[wt(('outo', m))], incs=[('O', 16)])
        mark(('outdma2', m), 'O')

    emit('gpsimd', lambda e: e.engine_nop(), waits=[('O', cnt['O'])])

    def runner(name):
        def _(eng):
            for fn, waits, incs in ops[name]:
                for s, v in waits:
                    eng.wait_ge(sems[s], v)
                inst = fn(eng)
                for s, a in incs:
                    inst = inst.then_inc(sems[s], a)
        return _

    with nc.Block() as block:
        block.sync(runner('sync'))
        block.gpsimd(runner('gpsimd'))
        block.tensor(runner('tensor'))
        block.vector(runner('vector'))
        block.scalar(runner('scalar'))

    for t in reversed(ctx):
        t.__exit__(None, None, None)
    return nc


def host_prep(inputs, T):
    emb = np.asarray(inputs['L0_attn_embeddings'], np.float32)
    B = emb.shape[0]

    def tile_w(WT):
        t = WT.reshape(G, 128, G, 128).transpose(2, 1, 0, 3).reshape(G, 128, 1024)
        return np.ascontiguousarray(t)

    wKt = tile_w(np.ascontiguousarray(np.asarray(inputs['L0_attn_W_K'], np.float32).T))
    wVt = tile_w(np.ascontiguousarray(np.asarray(inputs['L0_attn_W_V'], np.float32).T))
    wTdt = tile_w(np.ascontiguousarray(np.asarray(inputs['L0_attn_W_td'], np.float32).T))
    wRt = tile_w(np.ascontiguousarray(np.asarray(inputs['L0_attn_W_R'], np.float32).T))
    wFt = tile_w(np.asarray(inputs['L0_ffn_kernel'], np.float32))
    aTt = np.ascontiguousarray(np.asarray(inputs['L0_ffn_a'], np.float32).T)
    bmt = np.ascontiguousarray(np.asarray(inputs['L0_ffn_b'], np.float32))
    svec = np.ascontiguousarray(np.stack([
        np.asarray(inputs['L0_attn_mix'], np.float32),
        np.asarray(inputs['L0_attn_global_decay'], np.float32),
        np.asarray(inputs['L0_ffn_bias'], np.float32)], 0))
    in_maps = []
    for i in range(B):
        in_maps.append({
            'embT': np.ascontiguousarray(emb[i, :T].T),
            'wK': wKt, 'wV': wVt, 'wTd': wTdt, 'wR': wRt, 'wF': wFt,
            'aT': aTt, 'bm': bmt, 'svec': svec,
        })
    return in_maps


def kernel(**inputs):
    T = inputs['L0_attn_embeddings'].shape[1]
    nc = build_nc(T)
    in_maps = host_prep(inputs, T)
    res = run_bass_kernel_spmd(nc, in_maps, core_ids=list(range(NCORE)))
    outs = [np.ascontiguousarray(r['outT'].T) for r in res.results]
    return np.stack(outs, 0)


# version 11
# speedup vs baseline: 1.0936x; 1.0936x over previous
import numpy as np
import concourse.bass as bass
import concourse.mybir as mybir
from concourse.bass_utils import run_bass_kernel_spmd

F32 = mybir.dt.float32
F32R = mybir.dt.float32r
AF = mybir.ActivationFunctionType
OP = mybir.AluOpType

D = 1024
G = 8
NCORE = 8
RING = 4
TC = 256
MMDT = 'f32'

E10 = float(np.exp(np.float64(10.0)))
LOG2_128 = float(128 * np.log(np.float64(2.0)))
TINY = 1e-30

MSS = 8
MSL = TC // MSS
MSK = 32
MSJ = MSK + MSL
RTPW = MSK + TC
MSA0 = G * RTPW
MSB0 = MSA0 + G * MSS
SPKB0 = MSB0 + G * MSS

ENGS = ('sync', 'gpsimd', 'tensor', 'vector', 'scalar')


def build_nc(T):
    NCH = T // TC
    nc = bass.Bass("TRN2", target_bir_lowering=False, detect_race_conditions=False)

    embT = nc.dram_tensor("embT", [D, T], F32, kind="ExternalInput")
    wK = nc.dram_tensor("wK", [G, 128, 1024], F32, kind="ExternalInput")
    wV = nc.dram_tensor("wV", [G, 128, 1024], F32, kind="ExternalInput")
    wTd = nc.dram_tensor("wTd", [G, 128, 1024], F32, kind="ExternalInput")
    wR = nc.dram_tensor("wR", [G, 128, 1024], F32, kind="ExternalInput")
    wF = nc.dram_tensor("wF", [G, 128, 1024], F32, kind="ExternalInput")
    aT = nc.dram_tensor("aT", [16, 1024], F32, kind="ExternalInput")
    bm = nc.dram_tensor("bm", [16, 1024], F32, kind="ExternalInput")
    svec = nc.dram_tensor("svec", [3, 1024], F32, kind="ExternalInput")
    scr = nc.dram_tensor("scr", [128], F32, kind="Internal")
    wEff = nc.dram_tensor("wEff", [G, 128, 1024], F32, kind="Internal")
    outT = nc.dram_tensor("outT", [D, T], F32, kind="ExternalOutput")

    def dram_ap(h, off, dims):
        return bass.AP(h, off, [list(d) for d in dims])

    ctx = []

    def sb(name, shape, dtype=F32):
        t = nc.sbuf_tensor(name, shape, dtype)
        h = t.__enter__()
        ctx.append(t)
        return h

    def psum(name):
        t = nc.psum_tensor(name, [128, 512], F32)
        h = t.__enter__()
        ctx.append(t)
        return h

    wring = sb("wring", [128, RING * 1024])
    wq = sb("wq", [128, G * 1024])
    emb2 = [sb("embA", [128, G, TC + 1]), sb("embB", [128, G, TC + 1])]
    X3 = [sb("XA", [128, G, TC]), sb("XB", [128, G, TC]),
          sb("XC", [128, G, TC])]
    EK = sb("EK", [128, G, TC])
    EKV = sb("EKV", [128, G, TC])
    GEK = sb("GEK", [128, G, TC])
    VV = sb("VVb", [128, G, TC])
    TD = sb("TD", [128, G, TC])
    RT = sb("RT", [128, G, TC])
    Ab = sb("Ab", [128, G, 2, TC])
    Bb = sb("Bb", [128, G, TC])
    SPK = sb("SPK", [128, G, TC])
    FB = sb("FB", [128, G, TC])
    aTs = sb("aTs", [16, 1024])
    bs = sb("bs", [16, 1024])
    ones = sb("ones", [128, TC])
    st_mean = sb("st_mean", [1, 2 * TC])
    st_m2 = sb("st_m2", [1, 2 * TC])
    eps5 = sb("eps5", [1, 1])
    flv = sb("flv", [128, 8])
    fls = sb("fls", [128, 8])
    flg = sb("flg", [128, 8])
    smalls = {}
    for n in ('mixs', 'mx1m', 'gds', 'biass', 'disc', 't75', 'fsc', 'Lg',
              'epsf', 'thr', 'acar', 'bcar', 'fcar',
              'rr', 'lb', 'rexp', 'tin', 'bclip'):
        smalls[n] = sb(n, [128, G])
    S = type('S', (), smalls)
    wmax = sb("wmax", [128, 1])
    tmax = sb("tmax", [128, 1])
    scalec = sb("scalec", [128, 1])
    invc = sb("invc", [128, 1])
    trow = sb("trow", [1, 128])
    scal = sb("scal", [1, 1])
    pb = [psum(f"pb{i}") for i in range(8)]

    sems = {}
    for n in ('E', 'W', 'P', 'Q', 'M', 'SS', 'VN', 'O'):
        t = nc.semaphore(n)
        sems[n] = t.__enter__()
        ctx.append(t)

    ops = {e: [] for e in ENGS}
    cnt = {n: 0 for n in sems}
    marks = {}

    flushfn = {
        'vector': lambda e: e.tensor_copy(flv[:, 0:4], flv[:, 4:8]),
        'scalar': lambda e: e.activation(fls[:, 0:4], fls[:, 4:8], AF.Identity),
        'gpsimd': lambda e: e.tensor_tensor(flg[:, 0:4], flg[:, 4:8],
                                            flg[:, 4:8], OP.add),
    }

    def emit(eng, fn, waits=(), incs=(), flush=False):
        if waits and eng == 'vector':
            ops[eng].append((flushfn[eng], tuple(waits), ()))
            ops[eng].append((flushfn[eng], (), ()))
            waits = ()
        if flush and incs and eng in flushfn:
            ops[eng].append((fn, tuple(waits), ()))
            if eng == 'scalar':
                ops[eng].append((flushfn[eng], (), ()))
            ops[eng].append((flushfn[eng], (), tuple(incs)))
        else:
            ops[eng].append((fn, tuple(waits), tuple(incs)))
        for s, a in incs:
            cnt[s] += a

    def mark(key, sname):
        marks[key] = (sname, cnt[sname])

    def wt(key):
        return marks[key]

    def mm_dtype(ap):
        if MMDT == 'f32r':
            return ap.bitcast(F32R)
        return ap

    emit('gpsimd', lambda e: e.memset(ones[:, :], 1.0))
    emit('gpsimd', lambda e: e.memset(S.fsc[:, :], 1.0))
    emit('gpsimd', lambda e: e.memset(S.Lg[:, :], 0.0))
    for n in ('acar', 'bcar', 'fcar'):
        emit('gpsimd', lambda e, n=n: e.memset(smalls[n][:, :], 0.0))
    emit('gpsimd', lambda e: e.memset(emb2[0][:, :, 0:1], 0.0))
    emit('gpsimd', lambda e: e.memset(eps5[:, :], 1e-5))
    emit('gpsimd', lambda e: e.memset(flv[:, :], 0.0))
    emit('gpsimd', lambda e: e.memset(fls[:, :], 0.0))
    emit('gpsimd', lambda e: e.memset(flg[:, :], 0.0))
    emit('gpsimd', lambda e: e.memset(S.t75[:, :], float(2.0 ** -80)),
         incs=[('Q', 1)], flush=True)
    mark('msinit', 'Q')
    emit('vector', lambda e: e.tensor_mul(S.t75[:, :], S.t75[:, :], S.t75[:, :]),
         waits=[wt('msinit')])
    emit('vector', lambda e: e.tensor_single_scalar(
        S.disc[:, :], S.t75[:, :], 0.0, OP.is_equal))
    emit('vector', lambda e: e.tensor_scalar(
        S.epsf[:, :], S.fsc[:, :], 1e-6, TINY, OP.mult, OP.add))
    emit('vector', lambda e: e.tensor_scalar(
        S.tin[:, :], S.fsc[:, :], float(2.0 ** 64), float(2.0 ** 63),
        OP.mult, OP.min))
    emit('vector', flushfn['vector'])
    emit('vector', lambda e: e.tensor_scalar(
        S.thr[:, :], S.tin[:, :], float(2.0 ** 64), None, OP.mult),
        incs=[('VN', 1)], flush=True)
    mark('vinit', 'VN')

    for r, dst in ((0, S.mixs), (1, S.gds), (2, S.biass)):
        emit('gpsimd', (lambda r=r, dst=dst: lambda e: e.dma_start(
            out=dst[:, :], in_=dram_ap(svec, r * 1024, [[1, 128], [128, G]]),
            allow_slow_non_contiguous=True))(),
            incs=[('P', 16)])
    emit('gpsimd', lambda e: e.dma_start(out=aTs[:, :], in_=aT.ap()), incs=[('P', 16)])
    emit('gpsimd', lambda e: e.dma_start(out=bs[:, :], in_=bm.ap()), incs=[('P', 16)])
    mark('misc_dma', 'P')

    emit('scalar', lambda e: e.activation(S.gds[:, :], S.gds[:, :], AF.Exp),
         waits=[wt('misc_dma'), wt('vinit')], incs=[('SS', 1)], flush=True)
    mark('prolog_sc', 'SS')

    emit('vector', lambda e: e.tensor_scalar(
        S.mx1m[:, :], S.mixs[:, :], -1.0, 1.0, OP.mult, OP.add),
        waits=[wt('misc_dma')])

    for eb in range(G):
        waits = [wt(('amax', eb - RING))] if eb >= RING else []
        emit('sync', (lambda eb=eb: lambda e: e.dma_start(
            out=wring[:, (eb % RING) * 1024:(eb % RING + 1) * 1024],
            in_=wF.ap()[eb]))(), waits=waits, incs=[('W', 16)])
        mark(('wdma', eb), 'W')
        emit('vector', (lambda eb=eb: lambda e: e.tensor_reduce(
            tmax[:, :], wring[:, (eb % RING) * 1024:(eb % RING + 1) * 1024],
            mybir.AxisListType.X, OP.max, apply_absolute_value=True))(),
            waits=[wt(('wdma', eb))])
        emit('vector', flushfn['vector'])
        emit('vector', flushfn['vector'])
        if eb == 0:
            emit('vector', lambda e: e.tensor_copy(wmax[:, :], tmax[:, :]),
                 incs=[('VN', 1)], flush=True)
        else:
            emit('vector', lambda e: e.tensor_max(wmax[:, :], wmax[:, :], tmax[:, :]),
                 incs=[('VN', 1)], flush=True)
        mark(('amax', eb), 'VN')
    mark('absmax_done', 'VN')

    emit('gpsimd', lambda e: e.dma_start(
        out=dram_ap(scr, 0, [[1, 128]]), in_=wmax[:, :]),
        waits=[wt('absmax_done')], incs=[('P', 16)])
    mark('scr_out', 'P')
    emit('gpsimd', lambda e: e.dma_start(
        out=trow[:, :], in_=dram_ap(scr, 0, [[128, 1], [1, 128]])),
        waits=[wt('scr_out')], incs=[('P', 16)])
    mark('scr_in', 'P')
    emit('vector', lambda e: e.tensor_reduce(
        scal[:, :], trow[:, :], mybir.AxisListType.X, OP.max),
        waits=[wt('scr_in')], incs=[('VN', 1)], flush=True)
    mark('amax_all', 'VN')
    emit('tensor', lambda e: e.matmul(pb[4][:, 0:1], ones[0:1, 0:128],
                                      scal[:, :], start=True, stop=True),
         waits=[wt('amax_all')], incs=[('M', 1)])
    mark('bc_amax', 'M')
    emit('vector', lambda e: e.tensor_scalar(
        scalec[:, :], pb[4][:, 0:1], 1.0 / 7.0, None, OP.mult),
        waits=[wt('bc_amax')])
    for _ in range(6):
        emit('vector', flushfn['vector'])
    emit('vector', lambda e: e.reciprocal(invc[:, :], scalec[:, :]))
    emit('vector', flushfn['vector'])
    emit('vector', flushfn['vector'])
    emit('vector', lambda e: e.reciprocal(invc[:, :], scalec[:, :]))
    emit('vector', flushfn['vector'])
    emit('vector', flushfn['vector'])
    emit('vector', lambda e: e.reciprocal(invc[:, :], scalec[:, :]),
         incs=[('VN', 1)], flush=True)
    emit('vector', flushfn['vector'])
    mark('scale_ready', 'VN')

    def flat(buf, n=1024):
        return bass.AP(buf, 0, [[G * TC, 128], [1, n]])

    WTMP, QTMP = EK, TD
    for eb in range(G):
        w2 = G + eb
        waits = [wt(('weff', eb - RING))] if eb >= RING else [wt(('amax', G - 1))]
        emit('sync', (lambda eb=eb: lambda e: e.dma_start(
            out=wring[:, (eb % RING) * 1024:(eb % RING + 1) * 1024],
            in_=wF.ap()[eb]))(), waits=waits, incs=[('W', 16)])
        mark(('wdma', w2), 'W')
        slot = eb % RING
        wsl = wring[:, slot * 1024:(slot + 1) * 1024]
        emit('vector', (lambda wsl=wsl: lambda e: e.tensor_scalar(
            flat(WTMP), wsl, invc[:, 0:1], None, OP.mult))(),
            waits=[wt('scale_ready'), wt(('wdma', w2))])
        qwaits = [wt(('wq', eb - 1))] if eb > 0 else []
        emit('gpsimd', lambda e: e.memset(flat(QTMP), -8.0),
             waits=qwaits, incs=[('Q', 1)], flush=True)
        mark(('qms', eb), 'Q')
        for j in range(-8, 8):
            emit('vector', (lambda j=j: lambda e: e.scalar_tensor_tensor(
                flat(QTMP), flat(WTMP), j + 0.5, flat(QTMP), OP.is_gt, OP.add))(),
                waits=([wt(('qms', eb))] if j == -8 else []))
            emit('vector', flushfn['vector'])
        emit('vector', lambda e: e.tensor_single_scalar(
            flat(QTMP), flat(QTMP), 7.0, OP.min))
        emit('vector', flushfn['vector'])
        emit('vector', flushfn['vector'])
        emit('vector', (lambda eb=eb: lambda e: e.tensor_scalar(
            wq[:, eb * 1024:(eb + 1) * 1024], flat(QTMP), scalec[:, 0:1],
            None, OP.mult))(), incs=[('VN', 1)], flush=True)
        emit('vector', flushfn['vector'])
        mark(('wq', eb), 'VN')
        for kt in range(G):
            bank = pb[6] if kt < 4 else pb[7]
            col = (kt % 4) * 128
            emit('tensor', (lambda eb=eb, kt=kt, bank=bank, col=col: lambda e: e.matmul(
                bank[:, col:col + 128], aTs[:, kt * 128:(kt + 1) * 128],
                bs[:, eb * 128:(eb + 1) * 128], start=True, stop=True))(),
                waits=([wt('misc_dma'), wt(('wq', eb))] if kt == 0 else []),
                incs=([('M', 1)] if kt == G - 1 else []))
        mark(('ab', eb), 'M')
        for kt in range(G):
            bank = pb[6] if kt < 4 else pb[7]
            col = (kt % 4) * 128
            lo = eb * 1024 + kt * 128
            emit('vector', (lambda lo=lo, bank=bank, col=col: lambda e: e.tensor_add(
                wq[:, lo:lo + 128], wq[:, lo:lo + 128], bank[:, col:col + 128]))(),
                waits=([wt(('ab', eb))] if kt == 0 else []),
                incs=([('VN', 1)] if kt == G - 1 else []), flush=True)
        mark(('weff', eb), 'VN')
        emit('gpsimd', (lambda eb=eb: lambda e: e.dma_start(
            out=wEff.ap()[eb], in_=wq[:, eb * 1024:(eb + 1) * 1024]))(),
            waits=[wt(('weff', eb))], incs=[('P', 16)])
        mark(('weffd', eb), 'P')

    WROW = G * 1024

    def rtp_ap(j):
        return bass.AP(wq, j, [[WROW, 128], [RTPW, G], [MSL, MSS]])

    def rtp_xin(g):
        return wq[:, g * RTPW + MSK: g * RTPW + MSK + TC]

    def spk_ap(j):
        return bass.AP(wq, SPKB0 + j, [[WROW, 128], [MSS * MSJ, G], [MSJ, MSS]])

    def spk_out(g):
        return bass.AP(wq, SPKB0 + g * MSS * MSJ + MSK,
                       [[WROW, 128], [MSJ, MSS], [1, MSL]])

    STF0 = SPKB0 + G * MSS * MSJ
    st_rstd = wq[0:1, STF0:STF0 + 2 * TC]
    st_scr = wq[0:1, STF0 + 2 * TC:STF0 + 4 * TC]

    msA = wq[:, MSA0:MSA0 + G * MSS]
    msB = wq[:, MSB0:MSB0 + G * MSS]

    for g in range(G):
        emit('vector', (lambda g=g: lambda e: e.tensor_single_scalar(
            wq[:, g * RTPW: g * RTPW + MSK], ones[:, 0:MSK], 0.0, OP.mult))(),
            waits=([wt(('weffd', G - 1))] if g == 0 else []))

    PROJS = (('td', wTd, TD, AF.Exp, 1.0), ('v', wV, VV, AF.Copy, 1.0),
             ('k', wK, EK, AF.Exp, 1.0), ('r', wR, RT, AF.Exp, -1.0))
    wmarks = [None] * (2 * G)
    evseq = []
    state = {'widx': 2 * G}

    def wdma(src_ap, extra=()):
        i = state['widx']; state['widx'] += 1
        slot = i % RING
        dwaits = list(extra)
        prev = i - RING
        if prev >= 2 * G:
            dwaits.append(wt(wmarks[prev]))
        elif prev >= G:
            dwaits.append(wt(('weff', prev - G)))
        emit('sync', (lambda src_ap=src_ap, slot=slot: lambda e: e.dma_start(
            out=wring[:, slot * 1024:(slot + 1) * 1024], in_=src_ap))(),
            waits=dwaits, incs=[('W', 16)])
        mark(('wdma', i), 'W')
        return i, slot

    def proj_group(cc, p, eb, gate):
        pn, wh, dbuf, fn, sc = PROJS[p]
        i, slot = wdma(wh.ap()[eb])
        gi = len(evseq)
        bank = pb[gi % 4]
        Xc = X3[cc % 3]
        key = ('mmg', 'p', cc, p, eb)
        for k in range(G):
            waits = []
            if k == 0:
                waits.append(wt(('wdma', i)))
                waits.append(wt(('mix', cc)))
                if gi >= 3:
                    waits.append(wt(evseq[gi - 3]))
            emit('tensor', (lambda bank=bank, slot=slot, k=k, Xc=Xc: lambda e: e.matmul(
                bank[:, 0:TC],
                mm_dtype(wring[:, slot * 1024 + k * 128: slot * 1024 + (k + 1) * 128]),
                mm_dtype(Xc[:, k, :]), start=(k == 0), stop=(k == G - 1)))(),
                waits=waits, incs=([('M', 1)] if k == G - 1 else []))
        mark(key, 'M')
        wmarks.append(key)
        ewaits = [wt(key)]
        if gate is not None and eb == 0:
            ewaits.append(wt(gate))
        emit('scalar', (lambda bank=bank, dbuf=dbuf, fn=fn, sc=sc, eb=eb: lambda e:
            e.activation(dbuf[:, eb, :], bank[:, 0:TC], fn, scale=sc))(),
            waits=ewaits, incs=[('SS', 1)], flush=(eb == G - 1))
        ekey = ('evac', cc, p, eb)
        mark(ekey, 'SS')
        evseq.append(ekey)

    def ffn_group(m, eb):
        extra = [wt(('weffd', eb))] if m == 0 else []
        i, slot = wdma(wEff.ap()[eb], extra)
        gi = len(evseq)
        bank = pb[gi % 4]
        key = ('mmg', 'f', m, eb)
        for k in range(G):
            waits = [wt(('x1ng', m, k))]
            if k == 0:
                waits.append(wt(('wdma', i)))
                if gi >= 3:
                    waits.append(wt(evseq[gi - 3]))
            emit('tensor', (lambda bank=bank, slot=slot, k=k: lambda e: e.matmul(
                bank[:, 0:2 * TC],
                mm_dtype(wring[:, slot * 1024 + k * 128: slot * 1024 + (k + 1) * 128]),
                mm_dtype(Ab[:, k, :, :]), start=(k == 0), stop=(k == G - 1)))(),
                waits=waits, incs=([('M', 1)] if k == G - 1 else []))
        mark(key, 'M')
        wmarks.append(key)
        return key, bank

    def emit_embdma(cc):
        EMBb = emb2[cc % 2]
        waits = [wt(('x1', cc - 2))] if cc >= 2 else []
        if cc == 0:
            src = dram_ap(embT, 0, [[T, 128], [T * 128, G], [1, TC]])
            dst = EMBb[:, :, 1:TC + 1]
        else:
            src = dram_ap(embT, cc * TC - 1, [[T, 128], [T * 128, G], [1, TC + 1]])
            dst = EMBb[:, :, :]
        emit('sync', (lambda src=src, dst=dst: lambda e: e.dma_start(out=dst, in_=src))(),
             waits=waits, incs=[('E', 16)])
        mark(('embdma', cc), 'E')

    def emit_mix(cc):
        EMBb = emb2[cc % 2]
        Xc = X3[cc % 3]
        for g in range(G):
            waits = []
            if g == 0:
                waits.append(wt(('embdma', cc)))
                if cc == 0:
                    waits.append(wt('msinit'))
                if cc >= 3:
                    waits.append(wt(('sum1', 1, (cc - 3) // 2)))
                    waits.append(wt(('sq1', (cc - 3) // 2)))
            emit('vector', (lambda g=g, EMBb=EMBb, Xc=Xc: lambda e: e.tensor_scalar(
                Xc[:, g, :], EMBb[:, g, 1:TC + 1], S.mixs[:, g:g + 1], None, OP.mult))(),
                waits=waits)
            emit('vector', (lambda g=g, EMBb=EMBb, Xc=Xc: lambda e: e.scalar_tensor_tensor(
                Xc[:, g, :], EMBb[:, g, 0:TC], S.mx1m[:, g:g + 1], Xc[:, g, :],
                OP.mult, OP.add))(),
                incs=([('VN', 1)] if g == G - 1 else []), flush=True)
        mark(('mix', cc), 'VN')

    def ln_stats(tag, cc):
        W2 = 2 * TC
        emit('vector', lambda e: e.tensor_scalar(
            st_mean[:, :], pb[4][0:1, 0:W2], 1.0 / D, None, OP.mult),
            waits=[wt(('sum1', tag, cc))])
        emit('vector', flushfn['vector'])
        emit('vector', lambda e: e.tensor_scalar(
            st_m2[:, :], pb[5][0:1, 0:W2], 1.0 / D, None, OP.mult),
            waits=[wt(('sum2', tag, cc))])
        emit('vector', flushfn['vector'])
        emit('vector', lambda e: e.tensor_scalar(
            st_mean[:, :], pb[4][0:1, 0:W2], 1.0 / D, None, OP.mult))
        emit('vector', flushfn['vector'])
        emit('vector', lambda e: e.tensor_scalar(
            st_m2[:, :], pb[5][0:1, 0:W2], 1.0 / D, None, OP.mult))
        emit('vector', flushfn['vector'])
        emit('vector', lambda e: e.tensor_mul(st_scr, st_mean[:, :], st_mean[:, :]))
        emit('vector', flushfn['vector'])
        emit('vector', lambda e: e.tensor_sub(st_rstd, st_m2[:, :], st_scr),
             incs=[('VN', 1)], flush=True)
        mark(('var', tag, cc), 'VN')
        emit('scalar', flushfn['scalar'], waits=[wt(('var', tag, cc))])
        emit('scalar', flushfn['scalar'])
        emit('scalar', lambda e: e.activation(
            st_scr, st_rstd, AF.Ln, bias=eps5[:, :]))
        emit('scalar', flushfn['scalar'])
        emit('scalar', lambda e: e.activation(
            st_rstd, st_scr, AF.Exp, scale=-0.5),
            incs=[('SS', 1)], flush=True)
        mark(('rstd', tag, cc), 'SS')
        emit('vector', flushfn['vector'], waits=[wt(('rstd', tag, cc))])
        emit('vector', lambda e: e.tensor_mul(st_m2[:, :], st_mean[:, :], st_rstd),
             incs=[('VN', 1)], flush=True)
        mark(('m2', tag, cc), 'VN')
        emit('tensor', lambda e: e.matmul(
            pb[6][:, 0:W2], ones[0:1, 0:128], st_rstd, start=True, stop=True),
            waits=[wt(('m2', tag, cc))], incs=[('M', 1)])
        emit('tensor', lambda e: e.matmul(
            pb[7][:, 0:W2], ones[0:1, 0:128], st_m2[:, :], start=True, stop=True),
            incs=[('M', 1)])
        mark(('bcast', tag, cc), 'M')

    emit_embdma(0)
    emit_embdma(1)
    emit_mix(0)
    for p in range(4):
        for eb in range(G):
            proj_group(0, p, eb, None)

    for c in range(NCH):
        EMBc = emb2[c % 2]
        Xc = X3[c % 3]
        nx = c + 1
        last = (c == NCH - 1)

        if not last:
            emit_mix(nx)

        for g in range(G):
            emit('vector', (lambda g=g: lambda e: e.tensor_scalar(
                EK[:, g, :], EK[:, g, :], E10, S.fsc[:, g:g + 1], OP.min, OP.mult))(),
                waits=([wt(('evac', c, 2, G - 1))] if g == 0 else []))
        for g in range(G):
            waits = []
            if g == 0:
                waits.append(wt(('evac', c, 1, G - 1)))
                if c % 2 == 0 and c >= 2:
                    waits.append(wt(('sum1', 2, c // 2 - 1)))
                    waits.append(wt(('sq2', c // 2 - 1)))
                elif c % 2 == 1:
                    waits.append(wt(('rseed2', c - 1)))
            emit('vector', (lambda g=g: lambda e: e.tensor_mul(
                EKV[:, g, :], EK[:, g, :], VV[:, g, :]))(), waits=waits)
        for g in range(G):
            waits = []
            if g == 0:
                if c == 0:
                    waits.append(wt('prolog_sc'))
                elif c % 2 == 0:
                    waits.append(wt(('sum1', 2, c // 2 - 1)))
                    waits.append(wt(('sq2', c // 2 - 1)))
                else:
                    waits.append(wt(('rseed2', c - 1)))
            emit('vector', (lambda g=g: lambda e: e.tensor_scalar(
                GEK[:, g, :], EK[:, g, :], S.gds[:, g:g + 1], None, OP.mult))(),
                waits=waits)
        for g in range(G):
            waits = []
            if g == 0:
                waits.append(wt(('evac', c, 0, G - 1)))
                if c >= 2:
                    waits.append(wt(('mmg', 'f', c // 2 - 1, G - 1)))
            emit('vector', (lambda g=g, par=c % 2: lambda e: e.tensor_tensor_scan(
                Ab[:, g, par, :], TD[:, g, :], EKV[:, g, :], S.acar[:, g:g + 1],
                OP.mult, OP.add))(), waits=waits)
        for g in range(G):
            emit('vector', (lambda g=g: lambda e: e.tensor_tensor_scan(
                Bb[:, g, :], TD[:, g, :], EK[:, g, :], S.bcar[:, g:g + 1],
                OP.mult, OP.add))(),
                incs=([('VN', 1)] if g == G - 1 else []), flush=(g == G - 1))
        mark(('scdone', c), 'VN')
        for g in range(G):
            fwaits = []
            if g == 0 and c % 2 == 0 and c >= 2:
                fwaits.append(wt(('outdma2', c // 2 - 1)))
            emit('vector', (lambda g=g: lambda e: e.tensor_scalar(
                FB[:, g, :], Bb[:, g, :], S.thr[:, g:g + 1], S.disc[:, g:g + 1],
                OP.is_ge, OP.mult))(), waits=fwaits)
        for g in range(G):
            emit('vector', (lambda g=g: lambda e: e.tensor_tensor_scan(
                FB[:, g, :], ones[:, :], FB[:, g, :], S.fcar[:, g:g + 1],
                OP.mult, OP.max))())
        emit('vector', flushfn['vector'])
        emit('vector', flushfn['vector'])
        emit('vector', lambda e: e.tensor_copy(S.fcar[:, :], FB[:, :, TC - 1]))
        emit('vector', flushfn['vector'])

        if not last:
            for eb in range(G):
                proj_group(nx, 0, eb, ('scdone', c))

        emit('vector', lambda e: e.tensor_mul(
            VV[:, :, :], GEK[:, :, :], VV[:, :, :]))
        emit('vector', (lambda par=c % 2: lambda e: e.tensor_add(
            EK[:, :, :], VV[:, :, :], Ab[:, :, par, :]))(),
            incs=[('VN', 1)], flush=True)
        mark(('numdone', c), 'VN')
        for g in range(G):
            emit('vector', (lambda g=g: lambda e: e.scalar_tensor_tensor(
                GEK[:, g, :], Bb[:, g, :], S.epsf[:, g:g + 1], GEK[:, g, :],
                OP.add, OP.add))(),
                incs=([('VN', 1)] if g == G - 1 else []), flush=(g == G - 1))
        mark(('dendone', c), 'VN')
        if not last:
            emit('vector', lambda e: e.tensor_scalar(
                S.bclip[:, :], Bb[:, :, TC - 1], 1e-35, 1e18, OP.max, OP.min),
                incs=[('VN', 1)], flush=True)
            mark(('bclip', c), 'VN')

        emit('vector', lambda e: e.tensor_scalar(
            EKV[:, :, :], RT[:, :, :], 1.0, None, OP.add),
            waits=[wt(('evac', c, 3, G - 1))], incs=[('VN', 1)], flush=True)
        mark(('sigdone', c), 'VN')
        emit('vector', lambda e: e.tensor_scalar(
            FB[:, :, :], FB[:, :, :], -1.0, 1.0, OP.mult, OP.add))
        if c >= 1:
            emit('vector', lambda e: e.tensor_copy(
                bass.AP(wq, 0, [[WROW, 128], [RTPW, G], [1, MSK]]),
                bass.AP(wq, TC, [[WROW, 128], [RTPW, G], [1, MSK]])))

        swaits = [wt(('dendone', c))]
        if c % 2 == 0 and c >= 2:
            swaits.append(wt(('outdma1', c // 2 - 1)))
        emit('scalar', lambda e: e.activation(SPK[:, :, :], GEK[:, :, :], AF.Ln),
             waits=swaits)
        emit('scalar', (lambda Xc=Xc: lambda e: e.activation(
            Xc[:, :, :], SPK[:, :, :], AF.Exp, scale=-1.0))(),
            waits=[wt(('mmg', 'p', c, 3, G - 1))], incs=[('SS', 1)], flush=True)
        mark(('rseed1', c), 'SS')
        emit('scalar', lambda e: e.activation(SPK[:, :, :], EKV[:, :, :], AF.Ln),
             waits=[wt(('sigdone', c))])
        emit('scalar', lambda e: e.activation(
            RT[:, :, :], SPK[:, :, :], AF.Exp, scale=-1.0),
            incs=[('SS', 1)], flush=True)
        mark(('rseed2', c), 'SS')
        if not last:
            emit('scalar', lambda e: e.activation(S.lb[:, :], S.bclip[:, :], AF.Ln),
                 waits=[wt(('bclip', c))], incs=[('SS', 1)], flush=True)
            mark(('lnb', c), 'SS')

        emit('vector', (lambda Xc=Xc: lambda e: e.tensor_mul(
            SPK[:, :, :], GEK[:, :, :], Xc[:, :, :]))(),
            waits=[wt(('rseed2', c))])
        emit('vector', lambda e: e.tensor_scalar(
            SPK[:, :, :], SPK[:, :, :], -1.0, 2.0, OP.mult, OP.add))
        emit('vector', (lambda Xc=Xc: lambda e: e.tensor_mul(
            Xc[:, :, :], SPK[:, :, :], Xc[:, :, :]))())
        emit('vector', (lambda Xc=Xc: lambda e: e.tensor_mul(
            Xc[:, :, :], EK[:, :, :], Xc[:, :, :]))(),
            incs=[('VN', 1)], flush=True)
        mark(('wkvdone', c), 'VN')
        if not last:
            emit('vector', lambda e: e.tensor_scalar(
                S.rexp[:, :], S.lb[:, :], -1.0, None, OP.mult),
                waits=[wt(('lnb', c))])
            emit('vector', flushfn['vector'])
            emit('vector', lambda e: e.scalar_tensor_tensor(
                S.rexp[:, :], S.Lg[:, :], 42.0, S.rexp[:, :], OP.add, OP.min),
                incs=[('VN', 1)], flush=True)
            mark(('rexp', c), 'VN')
            emit('scalar', lambda e: e.activation(S.rr[:, :], S.rexp[:, :], AF.Exp),
                 waits=[wt(('rexp', c))], incs=[('SS', 1)], flush=True)
            mark(('rr', c), 'SS')

        if not last:
            for eb in range(G):
                proj_group(nx, 1, eb, ('numdone', c))

        emit('vector', lambda e: e.tensor_mul(
            EKV[:, :, :], EKV[:, :, :], RT[:, :, :]))
        emit('vector', lambda e: e.tensor_scalar(
            EKV[:, :, :], EKV[:, :, :], -1.0, 2.0, OP.mult, OP.add))
        emit('vector', lambda e: e.tensor_mul(
            SPK[:, :, :], EKV[:, :, :], RT[:, :, :]),
            incs=[('VN', 1)], flush=True)
        mark(('rnewt', c), 'VN')
        emit('vector', (lambda Xc=Xc: lambda e: e.tensor_mul(
            SPK[:, :, :], SPK[:, :, :], Xc[:, :, :]))())

        if not last:
            for eb in range(G):
                proj_group(nx, 2, eb, ('wkvdone', c))
            for eb in range(G):
                proj_group(nx, 3, eb, ('rnewt', c))

        emit('vector', lambda e: e.tensor_mul(
            bass.AP(wq, MSK, [[WROW, 128], [RTPW, G], [1, TC]]),
            SPK[:, :, :], FB[:, :, :]))

        if not last:
            emit('vector', (lambda par=c % 2: lambda e: e.tensor_mul(
                S.acar[:, :], Ab[:, :, par, TC - 1], S.rr[:, :]))(),
                waits=[wt(('rr', c))])
            emit('vector', lambda e: e.tensor_mul(
                S.bcar[:, :], Bb[:, :, TC - 1], S.rr[:, :]))
            emit('vector', lambda e: e.tensor_mul(S.fsc[:, :], S.fsc[:, :], S.rr[:, :]))
            emit('vector', lambda e: e.tensor_sub(S.Lg[:, :], S.Lg[:, :], S.rexp[:, :]))
            emit('vector', lambda e: e.tensor_scalar(
                S.epsf[:, :], S.fsc[:, :], 1e-6, TINY, OP.mult, OP.add))
            emit('vector', lambda e: e.tensor_scalar(
                S.tin[:, :], S.fsc[:, :], float(2.0 ** 64), float(2.0 ** 63),
                OP.mult, OP.min))
            emit('vector', flushfn['vector'])
            emit('vector', lambda e: e.tensor_scalar(
                S.thr[:, :], S.tin[:, :], float(2.0 ** 64), None, OP.mult))

        emit('vector', lambda e: e.tensor_copy(msB[:, :], rtp_ap(0)))
        emit('vector', lambda e: e.tensor_single_scalar(
            spk_ap(0), msB[:, :], 0.2, OP.is_gt))
        for j in range(1, MSJ):
            emit('vector', (lambda j=j: lambda e: e.scalar_tensor_tensor(
                msA[:, :], msB[:, :], 0.3, rtp_ap(j), OP.mult, OP.add))())
            emit('vector', (lambda j=j: lambda e: e.scalar_tensor_tensor(
                msB[:, :], spk_ap(j - 1), -0.2, msA[:, :], OP.mult, OP.add))())
            emit('vector', (lambda j=j: lambda e: e.tensor_single_scalar(
                spk_ap(j), msB[:, :], 0.2, OP.is_gt))())
        emit('vector', flushfn['vector'])
        emit('vector', flushfn['vector'])

        for g in range(G):
            emit('vector', (lambda g=g, EMBc=EMBc, Xc=Xc: lambda e: e.tensor_add(
                Xc[:, g, :], EMBc[:, g, 1:TC + 1], spk_out(g)))(),
                incs=([('VN', 1)] if g == G - 1 else []), flush=True)
        mark(('x1', c), 'VN')

        if c + 2 < NCH:
            emit_embdma(c + 2)

        if c % 2 == 0:
            continue

        m = c // 2
        ce = c - 1
        Xe = X3[ce % 3]
        Xo = X3[c % 3]
        W2 = 2 * TC

        emit('scalar', (lambda Xe=Xe: lambda e: e.activation(
            GEK[:, :, :], Xe[:, :, :], AF.Square))(),
            waits=[wt(('x1', c))])
        emit('scalar', (lambda Xo=Xo: lambda e: e.activation(
            EKV[:, :, :], Xo[:, :, :], AF.Square))(),
            incs=[('SS', 1)], flush=True)
        mark(('sq1', m), 'SS')
        for q, Xq in ((0, Xe), (1, Xo)):
            for g in range(G):
                waits = []
                if q == 0 and g == 0:
                    waits.append(wt(('x1', c)))
                    if m >= 1:
                        waits.append(wt(('stats2read', m - 1)))
                emit('tensor', (lambda g=g, q=q, Xq=Xq: lambda e: e.matmul(
                    pb[4][0:1, q * TC:(q + 1) * TC], ones[:, 0:1], Xq[:, g, :],
                    start=(g == 0), stop=(g == G - 1)))(),
                    waits=waits)
        emit('tensor', lambda e: e.matmul(
            pb[6][0:1, 0:1], ones[:, 0:1], ones[:, 0:1],
            start=True, stop=True), incs=[('M', 1)])
        mark(('sum1', 1, m), 'M')
        for q, SQ in ((0, GEK), (1, EKV)):
            for g in range(G):
                emit('tensor', (lambda g=g, q=q, SQ=SQ: lambda e: e.matmul(
                    pb[5][0:1, q * TC:(q + 1) * TC], ones[:, 0:1], SQ[:, g, :],
                    start=(g == 0), stop=(g == G - 1)))(),
                    waits=([wt(('sq1', m))] if (q == 0 and g == 0) else []))
        emit('tensor', lambda e: e.matmul(
            pb[6][0:1, 4:5], ones[:, 0:1], ones[:, 0:1],
            start=True, stop=True), incs=[('M', 1)])
        mark(('sum2', 1, m), 'M')
        ln_stats(1, m)
        for g in range(G):
            emit('vector', (lambda g=g, Xe=Xe: lambda e: e.tensor_mul(
                Ab[:, g, 0, :], Xe[:, g, :], pb[6][:, 0:TC]))(),
                waits=([wt(('bcast', 1, m))] if g == 0 else []))
            emit('vector', (lambda g=g: lambda e: e.tensor_sub(
                Ab[:, g, 0, :], Ab[:, g, 0, :], pb[7][:, 0:TC]))())
            emit('vector', (lambda g=g, Xo=Xo: lambda e: e.tensor_mul(
                Ab[:, g, 1, :], Xo[:, g, :], pb[6][:, TC:2 * TC]))())
            emit('vector', (lambda g=g: lambda e: e.tensor_sub(
                Ab[:, g, 1, :], Ab[:, g, 1, :], pb[7][:, TC:2 * TC]))(),
                incs=[('VN', 1)], flush=True)
            mark(('x1ng', m, g), 'VN')
        mark(('stats1read', m), 'VN')

        for eb in range(G):
            fkey, fbank = ffn_group(m, eb)
            waits = [wt(fkey)]
            if eb == 0:
                waits.append(wt(('sum2', 1, m)))
            emit('vector', (lambda eb=eb, fbank=fbank: lambda e: e.scalar_tensor_tensor(
                GEK[:, eb, :], fbank[:, 0:TC], S.biass[:, eb:eb + 1],
                Ab[:, eb, 0, :], OP.add, OP.add))(), waits=waits)
            emit('vector', (lambda eb=eb, fbank=fbank: lambda e: e.scalar_tensor_tensor(
                EKV[:, eb, :], fbank[:, TC:2 * TC], S.biass[:, eb:eb + 1],
                Ab[:, eb, 1, :], OP.add, OP.add))(),
                incs=[('VN', 1)], flush=True)
            fx2 = ('fx2', m, eb)
            mark(fx2, 'VN')
            evseq.append(fx2)
        mark(('x2', m), 'VN')

        emit('scalar', lambda e: e.activation(
            SPK[:, :, :], GEK[:, :, :], AF.Square),
            waits=[wt(('x2', m))])
        emit('scalar', lambda e: e.activation(
            FB[:, :, :], EKV[:, :, :], AF.Square),
            incs=[('SS', 1)], flush=True)
        mark(('sq2', m), 'SS')
        for q, XQ in ((0, GEK), (1, EKV)):
            for g in range(G):
                waits = []
                if q == 0 and g == 0:
                    waits.append(wt(('x2', m)))
                    waits.append(wt(('stats1read', m)))
                emit('tensor', (lambda g=g, q=q, XQ=XQ: lambda e: e.matmul(
                    pb[4][0:1, q * TC:(q + 1) * TC], ones[:, 0:1], XQ[:, g, :],
                    start=(g == 0), stop=(g == G - 1)))(),
                    waits=waits)
        emit('tensor', lambda e: e.matmul(
            pb[6][0:1, 8:9], ones[:, 0:1], ones[:, 0:1],
            start=True, stop=True), incs=[('M', 1)])
        mark(('sum1', 2, m), 'M')
        for q, SQ in ((0, SPK), (1, FB)):
            for g in range(G):
                emit('tensor', (lambda g=g, q=q, SQ=SQ: lambda e: e.matmul(
                    pb[5][0:1, q * TC:(q + 1) * TC], ones[:, 0:1], SQ[:, g, :],
                    start=(g == 0), stop=(g == G - 1)))(),
                    waits=([wt(('sq2', m))] if (q == 0 and g == 0) else []))
        emit('tensor', lambda e: e.matmul(
            pb[6][0:1, 12:13], ones[:, 0:1], ones[:, 0:1],
            start=True, stop=True), incs=[('M', 1)])
        mark(('sum2', 2, m), 'M')
        ln_stats(2, m)
        for g in range(G):
            emit('vector', (lambda g=g: lambda e: e.tensor_mul(
                SPK[:, g, :], GEK[:, g, :], pb[6][:, 0:TC]))(),
                waits=([wt(('bcast', 2, m))] if g == 0 else []))
            emit('vector', (lambda g=g: lambda e: e.tensor_sub(
                SPK[:, g, :], SPK[:, g, :], pb[7][:, 0:TC]))(),
                incs=([('VN', 1)] if g == G - 1 else []), flush=True)
        mark(('oute', m), 'VN')
        for g in range(G):
            emit('vector', (lambda g=g: lambda e: e.tensor_mul(
                FB[:, g, :], EKV[:, g, :], pb[6][:, TC:2 * TC]))())
            emit('vector', (lambda g=g: lambda e: e.tensor_sub(
                FB[:, g, :], FB[:, g, :], pb[7][:, TC:2 * TC]))(),
                incs=([('VN', 1)] if g == G - 1 else []), flush=True)
        mark(('outo', m), 'VN')
        mark(('stats2read', m), 'VN')

        emit('gpsimd', (lambda ce=ce: lambda e: e.dma_start(
            out=dram_ap(outT, ce * TC, [[T, 128], [T * 128, G], [1, TC]]),
            in_=SPK[:, :, :]))(),
            waits=[wt(('oute', m))], incs=[('O', 16)])
        mark(('outdma1', m), 'O')
        emit('gpsimd', (lambda c=c: lambda e: e.dma_start(
            out=dram_ap(outT, c * TC, [[T, 128], [T * 128, G], [1, TC]]),
            in_=FB[:, :, :]))(),
            waits=[wt(('outo', m))], incs=[('O', 16)])
        mark(('outdma2', m), 'O')

    emit('gpsimd', lambda e: e.engine_nop(), waits=[('O', cnt['O'])])

    def runner(name):
        def _(eng):
            for fn, waits, incs in ops[name]:
                for s, v in waits:
                    eng.wait_ge(sems[s], v)
                inst = fn(eng)
                for s, a in incs:
                    inst = inst.then_inc(sems[s], a)
        return _

    with nc.Block() as block:
        block.sync(runner('sync'))
        block.gpsimd(runner('gpsimd'))
        block.tensor(runner('tensor'))
        block.vector(runner('vector'))
        block.scalar(runner('scalar'))

    for t in reversed(ctx):
        t.__exit__(None, None, None)
    return nc


def host_prep(inputs, T):
    emb = np.asarray(inputs['L0_attn_embeddings'], np.float32)
    B = emb.shape[0]

    def tile_w(WT):
        t = WT.reshape(G, 128, G, 128).transpose(2, 1, 0, 3).reshape(G, 128, 1024)
        return np.ascontiguousarray(t)

    wKt = tile_w(np.ascontiguousarray(np.asarray(inputs['L0_attn_W_K'], np.float32).T))
    wVt = tile_w(np.ascontiguousarray(np.asarray(inputs['L0_attn_W_V'], np.float32).T))
    wTdt = tile_w(np.ascontiguousarray(np.asarray(inputs['L0_attn_W_td'], np.float32).T))
    wRt = tile_w(np.ascontiguousarray(np.asarray(inputs['L0_attn_W_R'], np.float32).T))
    wFt = tile_w(np.asarray(inputs['L0_ffn_kernel'], np.float32))
    aTt = np.ascontiguousarray(np.asarray(inputs['L0_ffn_a'], np.float32).T)
    bmt = np.ascontiguousarray(np.asarray(inputs['L0_ffn_b'], np.float32))
    svec = np.ascontiguousarray(np.stack([
        np.asarray(inputs['L0_attn_mix'], np.float32),
        np.asarray(inputs['L0_attn_global_decay'], np.float32),
        np.asarray(inputs['L0_ffn_bias'], np.float32)], 0))
    in_maps = []
    for i in range(B):
        in_maps.append({
            'embT': np.ascontiguousarray(emb[i, :T].T),
            'wK': wKt, 'wV': wVt, 'wTd': wTdt, 'wR': wRt, 'wF': wFt,
            'aT': aTt, 'bm': bmt, 'svec': svec,
        })
    return in_maps


def kernel(**inputs):
    T = inputs['L0_attn_embeddings'].shape[1]
    nc = build_nc(T)
    in_maps = host_prep(inputs, T)
    res = run_bass_kernel_spmd(nc, in_maps, core_ids=list(range(NCORE)))
    outs = [np.ascontiguousarray(r['outT'].T) for r in res.results]
    return np.stack(outs, 0)


# version 12 (final)
# speedup vs baseline: 1.1078x; 1.0129x over previous
import numpy as np
import concourse.bass as bass
import concourse.mybir as mybir
from concourse.bass_utils import run_bass_kernel_spmd

F32 = mybir.dt.float32
F32R = mybir.dt.float32r
AF = mybir.ActivationFunctionType
OP = mybir.AluOpType

D = 1024
G = 8
NCORE = 8
RING = 4
TC = 256
MMDT = 'f32'

E10 = float(np.exp(np.float64(10.0)))
LOG2_128 = float(128 * np.log(np.float64(2.0)))
TINY = 1e-30

MSS = 8
MSL = TC // MSS
MSK = 56
MSJ = MSK + MSL
MSW = MSL + 2
RTPW = MSK + TC
MSA0 = G * RTPW
MSB0 = MSA0 + G * MSS
SPKB0 = MSB0 + G * MSS

ENGS = ('sync', 'gpsimd', 'tensor', 'vector', 'scalar')


def build_nc(T):
    NCH = T // TC
    nc = bass.Bass("TRN2", target_bir_lowering=False, detect_race_conditions=False)

    embT = nc.dram_tensor("embT", [D, T], F32, kind="ExternalInput")
    wK = nc.dram_tensor("wK", [G, 128, 1024], F32, kind="ExternalInput")
    wV = nc.dram_tensor("wV", [G, 128, 1024], F32, kind="ExternalInput")
    wTd = nc.dram_tensor("wTd", [G, 128, 1024], F32, kind="ExternalInput")
    wR = nc.dram_tensor("wR", [G, 128, 1024], F32, kind="ExternalInput")
    wF = nc.dram_tensor("wF", [G, 128, 1024], F32, kind="ExternalInput")
    aT = nc.dram_tensor("aT", [16, 1024], F32, kind="ExternalInput")
    bm = nc.dram_tensor("bm", [16, 1024], F32, kind="ExternalInput")
    svec = nc.dram_tensor("svec", [3, 1024], F32, kind="ExternalInput")
    scr = nc.dram_tensor("scr", [128], F32, kind="Internal")
    wEff = nc.dram_tensor("wEff", [G, 128, 1024], F32, kind="Internal")
    outT = nc.dram_tensor("outT", [D, T], F32, kind="ExternalOutput")

    def dram_ap(h, off, dims):
        return bass.AP(h, off, [list(d) for d in dims])

    ctx = []

    def sb(name, shape, dtype=F32):
        t = nc.sbuf_tensor(name, shape, dtype)
        h = t.__enter__()
        ctx.append(t)
        return h

    def psum(name):
        t = nc.psum_tensor(name, [128, 512], F32)
        h = t.__enter__()
        ctx.append(t)
        return h

    wring = sb("wring", [128, RING * 1024])
    wq = sb("wq", [128, G * 1024])
    emb2 = [sb("embA", [128, G, TC + 1]), sb("embB", [128, G, TC + 1])]
    X3 = [sb("XA", [128, G, TC]), sb("XB", [128, G, TC]),
          sb("XC", [128, G, TC])]
    EK = sb("EK", [128, G, TC])
    EKV = sb("EKV", [128, G, TC])
    GEK = sb("GEK", [128, G, TC])
    VV = sb("VVb", [128, G, TC])
    TD = sb("TD", [128, G, TC])
    RT = sb("RT", [128, G, TC])
    Ab = sb("Ab", [128, G, 2, TC])
    Bb = sb("Bb", [128, G, TC])
    SPK = sb("SPK", [128, G, TC])
    FB = sb("FB", [128, G, TC])
    aTs = sb("aTs", [16, 1024])
    bs = sb("bs", [16, 1024])
    ones = sb("ones", [128, TC])
    st_mean = sb("st_mean", [1, 2 * TC])
    st_m2 = sb("st_m2", [1, 2 * TC])
    eps5 = sb("eps5", [1, 1])
    flv = sb("flv", [128, 8])
    fls = sb("fls", [128, 8])
    flg = sb("flg", [128, 8])
    smalls = {}
    for n in ('mixs', 'mx1m', 'gds', 'biass', 'disc', 't75', 'fsc', 'Lg',
              'epsf', 'thr', 'acar', 'bcar', 'fcar',
              'rr', 'lb', 'rexp', 'tin', 'bclip'):
        smalls[n] = sb(n, [128, G])
    S = type('S', (), smalls)
    wmax = sb("wmax", [128, 1])
    tmax = sb("tmax", [128, 1])
    scalec = sb("scalec", [128, 1])
    invc = sb("invc", [128, 1])
    trow = sb("trow", [1, 128])
    scal = sb("scal", [1, 1])
    pb = [psum(f"pb{i}") for i in range(8)]

    sems = {}
    for n in ('E', 'W', 'P', 'Q', 'M', 'SS', 'VN', 'O'):
        t = nc.semaphore(n)
        sems[n] = t.__enter__()
        ctx.append(t)

    ops = {e: [] for e in ENGS}
    cnt = {n: 0 for n in sems}
    marks = {}

    flushfn = {
        'vector': lambda e: e.tensor_copy(flv[:, 0:4], flv[:, 4:8]),
        'scalar': lambda e: e.activation(fls[:, 0:4], fls[:, 4:8], AF.Identity),
        'gpsimd': lambda e: e.tensor_tensor(flg[:, 0:4], flg[:, 4:8],
                                            flg[:, 4:8], OP.add),
    }

    def emit(eng, fn, waits=(), incs=(), flush=False):
        if waits and eng == 'vector':
            ops[eng].append((flushfn[eng], tuple(waits), ()))
            ops[eng].append((flushfn[eng], (), ()))
            waits = ()
        if flush and incs and eng in flushfn:
            ops[eng].append((fn, tuple(waits), ()))
            if eng == 'scalar':
                ops[eng].append((flushfn[eng], (), ()))
            ops[eng].append((flushfn[eng], (), tuple(incs)))
        else:
            ops[eng].append((fn, tuple(waits), tuple(incs)))
        for s, a in incs:
            cnt[s] += a

    def mark(key, sname):
        marks[key] = (sname, cnt[sname])

    def wt(key):
        return marks[key]

    def mm_dtype(ap):
        if MMDT == 'f32r':
            return ap.bitcast(F32R)
        return ap

    emit('gpsimd', lambda e: e.memset(ones[:, :], 1.0))
    emit('gpsimd', lambda e: e.memset(S.fsc[:, :], 1.0))
    emit('gpsimd', lambda e: e.memset(S.Lg[:, :], 0.0))
    for n in ('acar', 'bcar', 'fcar'):
        emit('gpsimd', lambda e, n=n: e.memset(smalls[n][:, :], 0.0))
    emit('gpsimd', lambda e: e.memset(emb2[0][:, :, 0:1], 0.0))
    emit('gpsimd', lambda e: e.memset(eps5[:, :], 1e-5))
    emit('gpsimd', lambda e: e.memset(flv[:, :], 0.0))
    emit('gpsimd', lambda e: e.memset(fls[:, :], 0.0))
    emit('gpsimd', lambda e: e.memset(flg[:, :], 0.0))
    emit('gpsimd', lambda e: e.memset(S.t75[:, :], float(2.0 ** -80)),
         incs=[('Q', 1)], flush=True)
    mark('msinit', 'Q')
    emit('vector', lambda e: e.tensor_mul(S.t75[:, :], S.t75[:, :], S.t75[:, :]),
         waits=[wt('msinit')])
    emit('vector', lambda e: e.tensor_single_scalar(
        S.disc[:, :], S.t75[:, :], 0.0, OP.is_equal))
    emit('vector', lambda e: e.tensor_scalar(
        S.epsf[:, :], S.fsc[:, :], 1e-6, TINY, OP.mult, OP.add))
    emit('vector', lambda e: e.tensor_scalar(
        S.tin[:, :], S.fsc[:, :], float(2.0 ** 64), float(2.0 ** 63),
        OP.mult, OP.min))
    emit('vector', flushfn['vector'])
    emit('vector', lambda e: e.tensor_scalar(
        S.thr[:, :], S.tin[:, :], float(2.0 ** 64), None, OP.mult),
        incs=[('VN', 1)], flush=True)
    mark('vinit', 'VN')

    for r, dst in ((0, S.mixs), (1, S.gds), (2, S.biass)):
        emit('gpsimd', (lambda r=r, dst=dst: lambda e: e.dma_start(
            out=dst[:, :], in_=dram_ap(svec, r * 1024, [[1, 128], [128, G]]),
            allow_slow_non_contiguous=True))(),
            incs=[('P', 16)])
    emit('gpsimd', lambda e: e.dma_start(out=aTs[:, :], in_=aT.ap()), incs=[('P', 16)])
    emit('gpsimd', lambda e: e.dma_start(out=bs[:, :], in_=bm.ap()), incs=[('P', 16)])
    mark('misc_dma', 'P')

    emit('scalar', lambda e: e.activation(S.gds[:, :], S.gds[:, :], AF.Exp),
         waits=[wt('misc_dma'), wt('vinit')], incs=[('SS', 1)], flush=True)
    mark('prolog_sc', 'SS')

    emit('vector', lambda e: e.tensor_scalar(
        S.mx1m[:, :], S.mixs[:, :], -1.0, 1.0, OP.mult, OP.add),
        waits=[wt('misc_dma')])

    for eb in range(G):
        waits = [wt(('amax', eb - RING))] if eb >= RING else []
        emit('sync', (lambda eb=eb: lambda e: e.dma_start(
            out=wring[:, (eb % RING) * 1024:(eb % RING + 1) * 1024],
            in_=wF.ap()[eb]))(), waits=waits, incs=[('W', 16)])
        mark(('wdma', eb), 'W')
        emit('vector', (lambda eb=eb: lambda e: e.tensor_reduce(
            tmax[:, :], wring[:, (eb % RING) * 1024:(eb % RING + 1) * 1024],
            mybir.AxisListType.X, OP.max, apply_absolute_value=True))(),
            waits=[wt(('wdma', eb))])
        emit('vector', flushfn['vector'])
        emit('vector', flushfn['vector'])
        if eb == 0:
            emit('vector', lambda e: e.tensor_copy(wmax[:, :], tmax[:, :]),
                 incs=[('VN', 1)], flush=True)
        else:
            emit('vector', lambda e: e.tensor_max(wmax[:, :], wmax[:, :], tmax[:, :]),
                 incs=[('VN', 1)], flush=True)
        mark(('amax', eb), 'VN')
    mark('absmax_done', 'VN')

    emit('gpsimd', lambda e: e.dma_start(
        out=dram_ap(scr, 0, [[1, 128]]), in_=wmax[:, :]),
        waits=[wt('absmax_done')], incs=[('P', 16)])
    mark('scr_out', 'P')
    emit('gpsimd', lambda e: e.dma_start(
        out=trow[:, :], in_=dram_ap(scr, 0, [[128, 1], [1, 128]])),
        waits=[wt('scr_out')], incs=[('P', 16)])
    mark('scr_in', 'P')
    emit('vector', lambda e: e.tensor_reduce(
        scal[:, :], trow[:, :], mybir.AxisListType.X, OP.max),
        waits=[wt('scr_in')], incs=[('VN', 1)], flush=True)
    mark('amax_all', 'VN')
    emit('tensor', lambda e: e.matmul(pb[4][:, 0:1], ones[0:1, 0:128],
                                      scal[:, :], start=True, stop=True),
         waits=[wt('amax_all')], incs=[('M', 1)])
    mark('bc_amax', 'M')
    emit('vector', lambda e: e.tensor_scalar(
        scalec[:, :], pb[4][:, 0:1], 1.0 / 7.0, None, OP.mult),
        waits=[wt('bc_amax')])
    for _ in range(6):
        emit('vector', flushfn['vector'])
    emit('vector', lambda e: e.reciprocal(invc[:, :], scalec[:, :]))
    emit('vector', flushfn['vector'])
    emit('vector', flushfn['vector'])
    emit('vector', lambda e: e.reciprocal(invc[:, :], scalec[:, :]))
    emit('vector', flushfn['vector'])
    emit('vector', flushfn['vector'])
    emit('vector', lambda e: e.reciprocal(invc[:, :], scalec[:, :]),
         incs=[('VN', 1)], flush=True)
    emit('vector', flushfn['vector'])
    mark('scale_ready', 'VN')

    def flat(buf, n=1024):
        return bass.AP(buf, 0, [[G * TC, 128], [1, n]])

    WTMP, QTMP = EK, TD
    for eb in range(G):
        w2 = G + eb
        waits = [wt(('weff', eb - RING))] if eb >= RING else [wt(('amax', G - 1))]
        emit('sync', (lambda eb=eb: lambda e: e.dma_start(
            out=wring[:, (eb % RING) * 1024:(eb % RING + 1) * 1024],
            in_=wF.ap()[eb]))(), waits=waits, incs=[('W', 16)])
        mark(('wdma', w2), 'W')
        slot = eb % RING
        wsl = wring[:, slot * 1024:(slot + 1) * 1024]
        emit('vector', (lambda wsl=wsl: lambda e: e.tensor_scalar(
            flat(WTMP), wsl, invc[:, 0:1], None, OP.mult))(),
            waits=[wt('scale_ready'), wt(('wdma', w2))])
        qwaits = [wt(('wq', eb - 1))] if eb > 0 else []
        emit('gpsimd', lambda e: e.memset(flat(QTMP), -8.0),
             waits=qwaits, incs=[('Q', 1)], flush=True)
        mark(('qms', eb), 'Q')
        for j in range(-8, 8):
            emit('vector', (lambda j=j: lambda e: e.scalar_tensor_tensor(
                flat(QTMP), flat(WTMP), j + 0.5, flat(QTMP), OP.is_gt, OP.add))(),
                waits=([wt(('qms', eb))] if j == -8 else []))
            emit('vector', flushfn['vector'])
        emit('vector', lambda e: e.tensor_single_scalar(
            flat(QTMP), flat(QTMP), 7.0, OP.min))
        emit('vector', flushfn['vector'])
        emit('vector', flushfn['vector'])
        emit('vector', (lambda eb=eb: lambda e: e.tensor_scalar(
            wq[:, eb * 1024:(eb + 1) * 1024], flat(QTMP), scalec[:, 0:1],
            None, OP.mult))(), incs=[('VN', 1)], flush=True)
        emit('vector', flushfn['vector'])
        mark(('wq', eb), 'VN')
        for kt in range(G):
            bank = pb[6] if kt < 4 else pb[7]
            col = (kt % 4) * 128
            emit('tensor', (lambda eb=eb, kt=kt, bank=bank, col=col: lambda e: e.matmul(
                bank[:, col:col + 128], aTs[:, kt * 128:(kt + 1) * 128],
                bs[:, eb * 128:(eb + 1) * 128], start=True, stop=True))(),
                waits=([wt('misc_dma'), wt(('wq', eb))] if kt == 0 else []),
                incs=([('M', 1)] if kt == G - 1 else []))
        mark(('ab', eb), 'M')
        for kt in range(G):
            bank = pb[6] if kt < 4 else pb[7]
            col = (kt % 4) * 128
            lo = eb * 1024 + kt * 128
            emit('vector', (lambda lo=lo, bank=bank, col=col: lambda e: e.tensor_add(
                wq[:, lo:lo + 128], wq[:, lo:lo + 128], bank[:, col:col + 128]))(),
                waits=([wt(('ab', eb))] if kt == 0 else []),
                incs=([('VN', 1)] if kt == G - 1 else []), flush=True)
        mark(('weff', eb), 'VN')
        emit('gpsimd', (lambda eb=eb: lambda e: e.dma_start(
            out=wEff.ap()[eb], in_=wq[:, eb * 1024:(eb + 1) * 1024]))(),
            waits=[wt(('weff', eb))], incs=[('P', 16)])
        mark(('weffd', eb), 'P')

    WROW = G * 1024

    def rtp_ap(j):
        return bass.AP(wq, j, [[WROW, 128], [RTPW, G], [MSL, MSS]])

    def rtp_xin(g):
        return wq[:, g * RTPW + MSK: g * RTPW + MSK + TC]

    def spk_col(j):
        return (j & 1) if j < MSK else (2 + j - MSK)

    def spk_ap(j):
        return bass.AP(wq, SPKB0 + spk_col(j),
                       [[WROW, 128], [MSS * MSW, G], [MSW, MSS]])

    def spk_out(g):
        return bass.AP(wq, SPKB0 + g * MSS * MSW + 2,
                       [[WROW, 128], [MSW, MSS], [1, MSL]])

    STF0 = SPKB0 + G * MSS * MSW
    st_rstd = wq[0:1, STF0:STF0 + 2 * TC]
    st_scr = wq[0:1, STF0 + 2 * TC:STF0 + 4 * TC]

    msA = wq[:, MSA0:MSA0 + G * MSS]
    msB = wq[:, MSB0:MSB0 + G * MSS]

    for g in range(G):
        emit('vector', (lambda g=g: lambda e: e.tensor_single_scalar(
            wq[:, g * RTPW: g * RTPW + MSK], ones[:, 0:MSK], 0.0, OP.mult))(),
            waits=([wt(('weffd', G - 1))] if g == 0 else []))

    PROJS = (('td', wTd, TD, AF.Exp, 1.0), ('v', wV, VV, AF.Copy, 1.0),
             ('k', wK, EK, AF.Exp, 1.0), ('r', wR, RT, AF.Exp, -1.0))
    wmarks = [None] * (2 * G)
    evseq = []
    state = {'widx': 2 * G}

    def wdma(src_ap, extra=()):
        i = state['widx']; state['widx'] += 1
        slot = i % RING
        dwaits = list(extra)
        prev = i - RING
        if prev >= 2 * G:
            dwaits.append(wt(wmarks[prev]))
        elif prev >= G:
            dwaits.append(wt(('weff', prev - G)))
        emit('sync', (lambda src_ap=src_ap, slot=slot: lambda e: e.dma_start(
            out=wring[:, slot * 1024:(slot + 1) * 1024], in_=src_ap))(),
            waits=dwaits, incs=[('W', 16)])
        mark(('wdma', i), 'W')
        return i, slot

    def proj_group(cc, p, eb, gate):
        pn, wh, dbuf, fn, sc = PROJS[p]
        i, slot = wdma(wh.ap()[eb])
        gi = len(evseq)
        bank = pb[gi % 4]
        Xc = X3[cc % 3]
        key = ('mmg', 'p', cc, p, eb)
        for k in range(G):
            waits = []
            if k == 0:
                waits.append(wt(('wdma', i)))
                waits.append(wt(('mix', cc)))
                if gi >= 3:
                    waits.append(wt(evseq[gi - 3]))
            emit('tensor', (lambda bank=bank, slot=slot, k=k, Xc=Xc: lambda e: e.matmul(
                bank[:, 0:TC],
                mm_dtype(wring[:, slot * 1024 + k * 128: slot * 1024 + (k + 1) * 128]),
                mm_dtype(Xc[:, k, :]), start=(k == 0), stop=(k == G - 1)))(),
                waits=waits, incs=([('M', 1)] if k == G - 1 else []))
        mark(key, 'M')
        wmarks.append(key)
        ewaits = [wt(key)]
        if gate is not None and eb == 0:
            ewaits.append(wt(gate))
        emit('scalar', (lambda bank=bank, dbuf=dbuf, fn=fn, sc=sc, eb=eb: lambda e:
            e.activation(dbuf[:, eb, :], bank[:, 0:TC], fn, scale=sc))(),
            waits=ewaits, incs=[('SS', 1)], flush=(eb == G - 1))
        ekey = ('evac', cc, p, eb)
        mark(ekey, 'SS')
        evseq.append(ekey)

    def ffn_group(m, eb):
        extra = [wt(('weffd', eb))] if m == 0 else []
        i, slot = wdma(wEff.ap()[eb], extra)
        gi = len(evseq)
        bank = pb[gi % 4]
        key = ('mmg', 'f', m, eb)
        for k in range(G):
            waits = [wt(('x1ng', m, k))]
            if k == 0:
                waits.append(wt(('wdma', i)))
                if gi >= 3:
                    waits.append(wt(evseq[gi - 3]))
            emit('tensor', (lambda bank=bank, slot=slot, k=k: lambda e: e.matmul(
                bank[:, 0:2 * TC],
                mm_dtype(wring[:, slot * 1024 + k * 128: slot * 1024 + (k + 1) * 128]),
                mm_dtype(Ab[:, k, :, :]), start=(k == 0), stop=(k == G - 1)))(),
                waits=waits, incs=([('M', 1)] if k == G - 1 else []))
        mark(key, 'M')
        wmarks.append(key)
        return key, bank

    def emit_embdma(cc):
        EMBb = emb2[cc % 2]
        waits = [wt(('x1', cc - 2))] if cc >= 2 else []
        if cc == 0:
            src = dram_ap(embT, 0, [[T, 128], [T * 128, G], [1, TC]])
            dst = EMBb[:, :, 1:TC + 1]
        else:
            src = dram_ap(embT, cc * TC - 1, [[T, 128], [T * 128, G], [1, TC + 1]])
            dst = EMBb[:, :, :]
        emit('sync', (lambda src=src, dst=dst: lambda e: e.dma_start(out=dst, in_=src))(),
             waits=waits, incs=[('E', 16)])
        mark(('embdma', cc), 'E')

    def emit_mix(cc):
        EMBb = emb2[cc % 2]
        Xc = X3[cc % 3]
        for g in range(G):
            waits = []
            if g == 0:
                waits.append(wt(('embdma', cc)))
                if cc == 0:
                    waits.append(wt('msinit'))
                if cc >= 3:
                    waits.append(wt(('sum1', 1, (cc - 3) // 2)))
                    waits.append(wt(('sq1', (cc - 3) // 2)))
            emit('vector', (lambda g=g, EMBb=EMBb, Xc=Xc: lambda e: e.tensor_scalar(
                Xc[:, g, :], EMBb[:, g, 1:TC + 1], S.mixs[:, g:g + 1], None, OP.mult))(),
                waits=waits)
            emit('vector', (lambda g=g, EMBb=EMBb, Xc=Xc: lambda e: e.scalar_tensor_tensor(
                Xc[:, g, :], EMBb[:, g, 0:TC], S.mx1m[:, g:g + 1], Xc[:, g, :],
                OP.mult, OP.add))(),
                incs=([('VN', 1)] if g == G - 1 else []), flush=True)
        mark(('mix', cc), 'VN')

    def ln_stats(tag, cc):
        W2 = 2 * TC
        emit('vector', lambda e: e.tensor_scalar(
            st_mean[:, :], pb[4][0:1, 0:W2], 1.0 / D, None, OP.mult),
            waits=[wt(('sum1', tag, cc))])
        emit('vector', flushfn['vector'])
        emit('vector', lambda e: e.tensor_scalar(
            st_m2[:, :], pb[5][0:1, 0:W2], 1.0 / D, None, OP.mult),
            waits=[wt(('sum2', tag, cc))])
        emit('vector', flushfn['vector'])
        emit('vector', lambda e: e.tensor_scalar(
            st_mean[:, :], pb[4][0:1, 0:W2], 1.0 / D, None, OP.mult))
        emit('vector', flushfn['vector'])
        emit('vector', lambda e: e.tensor_scalar(
            st_m2[:, :], pb[5][0:1, 0:W2], 1.0 / D, None, OP.mult))
        emit('vector', flushfn['vector'])
        emit('vector', lambda e: e.tensor_mul(st_scr, st_mean[:, :], st_mean[:, :]))
        emit('vector', flushfn['vector'])
        emit('vector', lambda e: e.tensor_sub(st_rstd, st_m2[:, :], st_scr),
             incs=[('VN', 1)], flush=True)
        mark(('var', tag, cc), 'VN')
        emit('scalar', flushfn['scalar'], waits=[wt(('var', tag, cc))])
        emit('scalar', flushfn['scalar'])
        emit('scalar', lambda e: e.activation(
            st_scr, st_rstd, AF.Ln, bias=eps5[:, :]))
        emit('scalar', flushfn['scalar'])
        emit('scalar', lambda e: e.activation(
            st_rstd, st_scr, AF.Exp, scale=-0.5),
            incs=[('SS', 1)], flush=True)
        mark(('rstd', tag, cc), 'SS')
        emit('vector', flushfn['vector'], waits=[wt(('rstd', tag, cc))])
        emit('vector', lambda e: e.tensor_mul(st_m2[:, :], st_mean[:, :], st_rstd),
             incs=[('VN', 1)], flush=True)
        mark(('m2', tag, cc), 'VN')
        emit('tensor', lambda e: e.matmul(
            pb[6][:, 0:W2], ones[0:1, 0:128], st_rstd, start=True, stop=True),
            waits=[wt(('m2', tag, cc))], incs=[('M', 1)])
        emit('tensor', lambda e: e.matmul(
            pb[7][:, 0:W2], ones[0:1, 0:128], st_m2[:, :], start=True, stop=True),
            incs=[('M', 1)])
        mark(('bcast', tag, cc), 'M')

    emit_embdma(0)
    emit_embdma(1)
    emit_mix(0)
    for p in range(4):
        for eb in range(G):
            proj_group(0, p, eb, None)

    for c in range(NCH):
        EMBc = emb2[c % 2]
        Xc = X3[c % 3]
        nx = c + 1
        last = (c == NCH - 1)

        if not last:
            emit_mix(nx)

        for g in range(G):
            emit('vector', (lambda g=g: lambda e: e.tensor_scalar(
                EK[:, g, :], EK[:, g, :], E10, S.fsc[:, g:g + 1], OP.min, OP.mult))(),
                waits=([wt(('evac', c, 2, G - 1))] if g == 0 else []))
        for g in range(G):
            waits = []
            if g == 0:
                waits.append(wt(('evac', c, 1, G - 1)))
                if c % 2 == 0 and c >= 2:
                    waits.append(wt(('sum1', 2, c // 2 - 1)))
                    waits.append(wt(('sq2', c // 2 - 1)))
                elif c % 2 == 1:
                    waits.append(wt(('rseed2', c - 1)))
            emit('vector', (lambda g=g: lambda e: e.tensor_mul(
                EKV[:, g, :], EK[:, g, :], VV[:, g, :]))(), waits=waits)
        for g in range(G):
            waits = []
            if g == 0:
                if c == 0:
                    waits.append(wt('prolog_sc'))
                elif c % 2 == 0:
                    waits.append(wt(('sum1', 2, c // 2 - 1)))
                    waits.append(wt(('sq2', c // 2 - 1)))
                else:
                    waits.append(wt(('rseed2', c - 1)))
            emit('vector', (lambda g=g: lambda e: e.tensor_scalar(
                GEK[:, g, :], EK[:, g, :], S.gds[:, g:g + 1], None, OP.mult))(),
                waits=waits)
        for g in range(G):
            waits = []
            if g == 0:
                waits.append(wt(('evac', c, 0, G - 1)))
                if c >= 2:
                    waits.append(wt(('mmg', 'f', c // 2 - 1, G - 1)))
            emit('vector', (lambda g=g, par=c % 2: lambda e: e.tensor_tensor_scan(
                Ab[:, g, par, :], TD[:, g, :], EKV[:, g, :], S.acar[:, g:g + 1],
                OP.mult, OP.add))(), waits=waits)
        for g in range(G):
            emit('vector', (lambda g=g: lambda e: e.tensor_tensor_scan(
                Bb[:, g, :], TD[:, g, :], EK[:, g, :], S.bcar[:, g:g + 1],
                OP.mult, OP.add))(),
                incs=([('VN', 1)] if g == G - 1 else []), flush=(g == G - 1))
        mark(('scdone', c), 'VN')
        for g in range(G):
            fwaits = []
            if g == 0 and c % 2 == 0 and c >= 2:
                fwaits.append(wt(('outdma2', c // 2 - 1)))
            emit('vector', (lambda g=g: lambda e: e.tensor_scalar(
                FB[:, g, :], Bb[:, g, :], S.thr[:, g:g + 1], S.disc[:, g:g + 1],
                OP.is_ge, OP.mult))(), waits=fwaits)
        for g in range(G):
            emit('vector', (lambda g=g: lambda e: e.tensor_tensor_scan(
                FB[:, g, :], ones[:, :], FB[:, g, :], S.fcar[:, g:g + 1],
                OP.mult, OP.max))())
        emit('vector', flushfn['vector'])
        emit('vector', flushfn['vector'])
        emit('vector', lambda e: e.tensor_copy(S.fcar[:, :], FB[:, :, TC - 1]))
        emit('vector', flushfn['vector'])

        if not last:
            for eb in range(G):
                proj_group(nx, 0, eb, ('scdone', c))

        emit('vector', lambda e: e.tensor_mul(
            VV[:, :, :], GEK[:, :, :], VV[:, :, :]))
        emit('vector', (lambda par=c % 2: lambda e: e.tensor_add(
            EK[:, :, :], VV[:, :, :], Ab[:, :, par, :]))(),
            incs=[('VN', 1)], flush=True)
        mark(('numdone', c), 'VN')
        for g in range(G):
            emit('vector', (lambda g=g: lambda e: e.scalar_tensor_tensor(
                GEK[:, g, :], Bb[:, g, :], S.epsf[:, g:g + 1], GEK[:, g, :],
                OP.add, OP.add))(),
                incs=([('VN', 1)] if g == G - 1 else []), flush=(g == G - 1))
        mark(('dendone', c), 'VN')
        if not last:
            emit('vector', lambda e: e.tensor_scalar(
                S.bclip[:, :], Bb[:, :, TC - 1], 1e-35, 1e18, OP.max, OP.min),
                incs=[('VN', 1)], flush=True)
            mark(('bclip', c), 'VN')

        emit('vector', lambda e: e.tensor_scalar(
            EKV[:, :, :], RT[:, :, :], 1.0, None, OP.add),
            waits=[wt(('evac', c, 3, G - 1))], incs=[('VN', 1)], flush=True)
        mark(('sigdone', c), 'VN')
        emit('vector', lambda e: e.tensor_scalar(
            FB[:, :, :], FB[:, :, :], -1.0, 1.0, OP.mult, OP.add))
        if c >= 1:
            emit('vector', lambda e: e.tensor_copy(
                bass.AP(wq, 0, [[WROW, 128], [RTPW, G], [1, MSK]]),
                bass.AP(wq, TC, [[WROW, 128], [RTPW, G], [1, MSK]])))

        swaits = [wt(('dendone', c))]
        if c % 2 == 0 and c >= 2:
            swaits.append(wt(('outdma1', c // 2 - 1)))
        emit('scalar', lambda e: e.activation(SPK[:, :, :], GEK[:, :, :], AF.Ln),
             waits=swaits)
        emit('scalar', (lambda Xc=Xc: lambda e: e.activation(
            Xc[:, :, :], SPK[:, :, :], AF.Exp, scale=-1.0))(),
            waits=[wt(('mmg', 'p', c, 3, G - 1))], incs=[('SS', 1)], flush=True)
        mark(('rseed1', c), 'SS')
        emit('scalar', lambda e: e.activation(SPK[:, :, :], EKV[:, :, :], AF.Ln),
             waits=[wt(('sigdone', c))])
        emit('scalar', lambda e: e.activation(
            RT[:, :, :], SPK[:, :, :], AF.Exp, scale=-1.0),
            incs=[('SS', 1)], flush=True)
        mark(('rseed2', c), 'SS')
        if not last:
            emit('scalar', lambda e: e.activation(S.lb[:, :], S.bclip[:, :], AF.Ln),
                 waits=[wt(('bclip', c))], incs=[('SS', 1)], flush=True)
            mark(('lnb', c), 'SS')

        emit('vector', (lambda Xc=Xc: lambda e: e.tensor_mul(
            SPK[:, :, :], GEK[:, :, :], Xc[:, :, :]))(),
            waits=[wt(('rseed2', c))])
        emit('vector', lambda e: e.tensor_scalar(
            SPK[:, :, :], SPK[:, :, :], -1.0, 2.0, OP.mult, OP.add))
        emit('vector', (lambda Xc=Xc: lambda e: e.tensor_mul(
            Xc[:, :, :], SPK[:, :, :], Xc[:, :, :]))())
        emit('vector', (lambda Xc=Xc: lambda e: e.tensor_mul(
            Xc[:, :, :], EK[:, :, :], Xc[:, :, :]))(),
            incs=[('VN', 1)], flush=True)
        mark(('wkvdone', c), 'VN')
        if not last:
            emit('vector', lambda e: e.tensor_scalar(
                S.rexp[:, :], S.lb[:, :], -1.0, None, OP.mult),
                waits=[wt(('lnb', c))])
            emit('vector', flushfn['vector'])
            emit('vector', lambda e: e.scalar_tensor_tensor(
                S.rexp[:, :], S.Lg[:, :], 42.0, S.rexp[:, :], OP.add, OP.min),
                incs=[('VN', 1)], flush=True)
            mark(('rexp', c), 'VN')
            emit('scalar', lambda e: e.activation(S.rr[:, :], S.rexp[:, :], AF.Exp),
                 waits=[wt(('rexp', c))], incs=[('SS', 1)], flush=True)
            mark(('rr', c), 'SS')

        if not last:
            for eb in range(G):
                proj_group(nx, 1, eb, ('numdone', c))

        emit('vector', lambda e: e.tensor_mul(
            EKV[:, :, :], EKV[:, :, :], RT[:, :, :]))
        emit('vector', lambda e: e.tensor_scalar(
            EKV[:, :, :], EKV[:, :, :], -1.0, 2.0, OP.mult, OP.add))
        emit('vector', lambda e: e.tensor_mul(
            SPK[:, :, :], EKV[:, :, :], RT[:, :, :]),
            incs=[('VN', 1)], flush=True)
        mark(('rnewt', c), 'VN')
        emit('vector', (lambda Xc=Xc: lambda e: e.tensor_mul(
            SPK[:, :, :], SPK[:, :, :], Xc[:, :, :]))())

        if not last:
            for eb in range(G):
                proj_group(nx, 2, eb, ('wkvdone', c))
            for eb in range(G):
                proj_group(nx, 3, eb, ('rnewt', c))

        emit('vector', lambda e: e.tensor_mul(
            bass.AP(wq, MSK, [[WROW, 128], [RTPW, G], [1, TC]]),
            SPK[:, :, :], FB[:, :, :]))

        if not last:
            emit('vector', (lambda par=c % 2: lambda e: e.tensor_mul(
                S.acar[:, :], Ab[:, :, par, TC - 1], S.rr[:, :]))(),
                waits=[wt(('rr', c))])
            emit('vector', lambda e: e.tensor_mul(
                S.bcar[:, :], Bb[:, :, TC - 1], S.rr[:, :]))
            emit('vector', lambda e: e.tensor_mul(S.fsc[:, :], S.fsc[:, :], S.rr[:, :]))
            emit('vector', lambda e: e.tensor_sub(S.Lg[:, :], S.Lg[:, :], S.rexp[:, :]))
            emit('vector', lambda e: e.tensor_scalar(
                S.epsf[:, :], S.fsc[:, :], 1e-6, TINY, OP.mult, OP.add))
            emit('vector', lambda e: e.tensor_scalar(
                S.tin[:, :], S.fsc[:, :], float(2.0 ** 64), float(2.0 ** 63),
                OP.mult, OP.min))
            emit('vector', flushfn['vector'])
            emit('vector', lambda e: e.tensor_scalar(
                S.thr[:, :], S.tin[:, :], float(2.0 ** 64), None, OP.mult))

        emit('vector', lambda e: e.tensor_copy(msB[:, :], rtp_ap(0)))
        emit('vector', lambda e: e.tensor_single_scalar(
            spk_ap(0), msB[:, :], 0.2, OP.is_gt))
        for j in range(1, MSJ):
            emit('vector', (lambda j=j: lambda e: e.scalar_tensor_tensor(
                msA[:, :], msB[:, :], 0.3, rtp_ap(j), OP.mult, OP.add))())
            emit('vector', (lambda j=j: lambda e: e.scalar_tensor_tensor(
                msB[:, :], spk_ap(j - 1), -0.2, msA[:, :], OP.mult, OP.add))())
            emit('vector', (lambda j=j: lambda e: e.tensor_single_scalar(
                spk_ap(j), msB[:, :], 0.2, OP.is_gt))())
        emit('vector', flushfn['vector'])
        emit('vector', flushfn['vector'])

        for g in range(G):
            emit('vector', (lambda g=g, EMBc=EMBc, Xc=Xc: lambda e: e.tensor_add(
                Xc[:, g, :], EMBc[:, g, 1:TC + 1], spk_out(g)))(),
                incs=([('VN', 1)] if g == G - 1 else []), flush=True)
        mark(('x1', c), 'VN')

        if c + 2 < NCH:
            emit_embdma(c + 2)

        if c % 2 == 0:
            continue

        m = c // 2
        ce = c - 1
        Xe = X3[ce % 3]
        Xo = X3[c % 3]
        W2 = 2 * TC

        emit('scalar', (lambda Xe=Xe: lambda e: e.activation(
            GEK[:, :, :], Xe[:, :, :], AF.Square))(),
            waits=[wt(('x1', c))])
        emit('scalar', (lambda Xo=Xo: lambda e: e.activation(
            EKV[:, :, :], Xo[:, :, :], AF.Square))(),
            incs=[('SS', 1)], flush=True)
        mark(('sq1', m), 'SS')
        for q, Xq in ((0, Xe), (1, Xo)):
            for g in range(G):
                waits = []
                if q == 0 and g == 0:
                    waits.append(wt(('x1', c)))
                    if m >= 1:
                        waits.append(wt(('stats2read', m - 1)))
                emit('tensor', (lambda g=g, q=q, Xq=Xq: lambda e: e.matmul(
                    pb[4][0:1, q * TC:(q + 1) * TC], ones[:, 0:1], Xq[:, g, :],
                    start=(g == 0), stop=(g == G - 1)))(),
                    waits=waits)
        emit('tensor', lambda e: e.matmul(
            pb[6][0:1, 0:1], ones[:, 0:1], ones[:, 0:1],
            start=True, stop=True), incs=[('M', 1)])
        mark(('sum1', 1, m), 'M')
        for q, SQ in ((0, GEK), (1, EKV)):
            for g in range(G):
                emit('tensor', (lambda g=g, q=q, SQ=SQ: lambda e: e.matmul(
                    pb[5][0:1, q * TC:(q + 1) * TC], ones[:, 0:1], SQ[:, g, :],
                    start=(g == 0), stop=(g == G - 1)))(),
                    waits=([wt(('sq1', m))] if (q == 0 and g == 0) else []))
        emit('tensor', lambda e: e.matmul(
            pb[6][0:1, 4:5], ones[:, 0:1], ones[:, 0:1],
            start=True, stop=True), incs=[('M', 1)])
        mark(('sum2', 1, m), 'M')
        ln_stats(1, m)
        for g in range(G):
            emit('vector', (lambda g=g, Xe=Xe: lambda e: e.tensor_mul(
                Ab[:, g, 0, :], Xe[:, g, :], pb[6][:, 0:TC]))(),
                waits=([wt(('bcast', 1, m))] if g == 0 else []))
            emit('vector', (lambda g=g: lambda e: e.tensor_sub(
                Ab[:, g, 0, :], Ab[:, g, 0, :], pb[7][:, 0:TC]))())
            emit('vector', (lambda g=g, Xo=Xo: lambda e: e.tensor_mul(
                Ab[:, g, 1, :], Xo[:, g, :], pb[6][:, TC:2 * TC]))())
            emit('vector', (lambda g=g: lambda e: e.tensor_sub(
                Ab[:, g, 1, :], Ab[:, g, 1, :], pb[7][:, TC:2 * TC]))(),
                incs=[('VN', 1)], flush=True)
            mark(('x1ng', m, g), 'VN')
        mark(('stats1read', m), 'VN')

        for eb in range(G):
            fkey, fbank = ffn_group(m, eb)
            waits = [wt(fkey)]
            if eb == 0:
                waits.append(wt(('sum2', 1, m)))
            emit('vector', (lambda eb=eb, fbank=fbank: lambda e: e.scalar_tensor_tensor(
                GEK[:, eb, :], fbank[:, 0:TC], S.biass[:, eb:eb + 1],
                Ab[:, eb, 0, :], OP.add, OP.add))(), waits=waits)
            emit('vector', (lambda eb=eb, fbank=fbank: lambda e: e.scalar_tensor_tensor(
                EKV[:, eb, :], fbank[:, TC:2 * TC], S.biass[:, eb:eb + 1],
                Ab[:, eb, 1, :], OP.add, OP.add))(),
                incs=[('VN', 1)], flush=True)
            fx2 = ('fx2', m, eb)
            mark(fx2, 'VN')
            evseq.append(fx2)
        mark(('x2', m), 'VN')

        emit('scalar', lambda e: e.activation(
            SPK[:, :, :], GEK[:, :, :], AF.Square),
            waits=[wt(('x2', m))])
        emit('scalar', lambda e: e.activation(
            FB[:, :, :], EKV[:, :, :], AF.Square),
            incs=[('SS', 1)], flush=True)
        mark(('sq2', m), 'SS')
        for q, XQ in ((0, GEK), (1, EKV)):
            for g in range(G):
                waits = []
                if q == 0 and g == 0:
                    waits.append(wt(('x2', m)))
                    waits.append(wt(('stats1read', m)))
                emit('tensor', (lambda g=g, q=q, XQ=XQ: lambda e: e.matmul(
                    pb[4][0:1, q * TC:(q + 1) * TC], ones[:, 0:1], XQ[:, g, :],
                    start=(g == 0), stop=(g == G - 1)))(),
                    waits=waits)
        emit('tensor', lambda e: e.matmul(
            pb[6][0:1, 8:9], ones[:, 0:1], ones[:, 0:1],
            start=True, stop=True), incs=[('M', 1)])
        mark(('sum1', 2, m), 'M')
        for q, SQ in ((0, SPK), (1, FB)):
            for g in range(G):
                emit('tensor', (lambda g=g, q=q, SQ=SQ: lambda e: e.matmul(
                    pb[5][0:1, q * TC:(q + 1) * TC], ones[:, 0:1], SQ[:, g, :],
                    start=(g == 0), stop=(g == G - 1)))(),
                    waits=([wt(('sq2', m))] if (q == 0 and g == 0) else []))
        emit('tensor', lambda e: e.matmul(
            pb[6][0:1, 12:13], ones[:, 0:1], ones[:, 0:1],
            start=True, stop=True), incs=[('M', 1)])
        mark(('sum2', 2, m), 'M')
        ln_stats(2, m)
        for g in range(G):
            emit('vector', (lambda g=g: lambda e: e.tensor_mul(
                SPK[:, g, :], GEK[:, g, :], pb[6][:, 0:TC]))(),
                waits=([wt(('bcast', 2, m))] if g == 0 else []))
            emit('vector', (lambda g=g: lambda e: e.tensor_sub(
                SPK[:, g, :], SPK[:, g, :], pb[7][:, 0:TC]))(),
                incs=([('VN', 1)] if g == G - 1 else []), flush=True)
        mark(('oute', m), 'VN')
        for g in range(G):
            emit('vector', (lambda g=g: lambda e: e.tensor_mul(
                FB[:, g, :], EKV[:, g, :], pb[6][:, TC:2 * TC]))())
            emit('vector', (lambda g=g: lambda e: e.tensor_sub(
                FB[:, g, :], FB[:, g, :], pb[7][:, TC:2 * TC]))(),
                incs=([('VN', 1)] if g == G - 1 else []), flush=True)
        mark(('outo', m), 'VN')
        mark(('stats2read', m), 'VN')

        emit('gpsimd', (lambda ce=ce: lambda e: e.dma_start(
            out=dram_ap(outT, ce * TC, [[T, 128], [T * 128, G], [1, TC]]),
            in_=SPK[:, :, :]))(),
            waits=[wt(('oute', m))], incs=[('O', 16)])
        mark(('outdma1', m), 'O')
        emit('gpsimd', (lambda c=c: lambda e: e.dma_start(
            out=dram_ap(outT, c * TC, [[T, 128], [T * 128, G], [1, TC]]),
            in_=FB[:, :, :]))(),
            waits=[wt(('outo', m))], incs=[('O', 16)])
        mark(('outdma2', m), 'O')

    emit('gpsimd', lambda e: e.engine_nop(), waits=[('O', cnt['O'])])

    def runner(name):
        def _(eng):
            for fn, waits, incs in ops[name]:
                for s, v in waits:
                    eng.wait_ge(sems[s], v)
                inst = fn(eng)
                for s, a in incs:
                    inst = inst.then_inc(sems[s], a)
        return _

    with nc.Block() as block:
        block.sync(runner('sync'))
        block.gpsimd(runner('gpsimd'))
        block.tensor(runner('tensor'))
        block.vector(runner('vector'))
        block.scalar(runner('scalar'))

    for t in reversed(ctx):
        t.__exit__(None, None, None)
    return nc


def host_prep(inputs, T):
    emb = np.asarray(inputs['L0_attn_embeddings'], np.float32)
    B = emb.shape[0]

    def tile_w(WT):
        t = WT.reshape(G, 128, G, 128).transpose(2, 1, 0, 3).reshape(G, 128, 1024)
        return np.ascontiguousarray(t)

    wKt = tile_w(np.ascontiguousarray(np.asarray(inputs['L0_attn_W_K'], np.float32).T))
    wVt = tile_w(np.ascontiguousarray(np.asarray(inputs['L0_attn_W_V'], np.float32).T))
    wTdt = tile_w(np.ascontiguousarray(np.asarray(inputs['L0_attn_W_td'], np.float32).T))
    wRt = tile_w(np.ascontiguousarray(np.asarray(inputs['L0_attn_W_R'], np.float32).T))
    wFt = tile_w(np.asarray(inputs['L0_ffn_kernel'], np.float32))
    aTt = np.ascontiguousarray(np.asarray(inputs['L0_ffn_a'], np.float32).T)
    bmt = np.ascontiguousarray(np.asarray(inputs['L0_ffn_b'], np.float32))
    svec = np.ascontiguousarray(np.stack([
        np.asarray(inputs['L0_attn_mix'], np.float32),
        np.asarray(inputs['L0_attn_global_decay'], np.float32),
        np.asarray(inputs['L0_ffn_bias'], np.float32)], 0))
    in_maps = []
    for i in range(B):
        in_maps.append({
            'embT': np.ascontiguousarray(emb[i, :T].T),
            'wK': wKt, 'wV': wVt, 'wTd': wTdt, 'wR': wRt, 'wF': wFt,
            'aT': aTt, 'bm': bmt, 'svec': svec,
        })
    return in_maps


def kernel(**inputs):
    T = inputs['L0_attn_embeddings'].shape[1]
    nc = build_nc(T)
    in_maps = host_prep(inputs, T)
    res = run_bass_kernel_spmd(nc, in_maps, core_ids=list(range(NCORE)))
    outs = [np.ascontiguousarray(r['outT'].T) for r in res.results]
    return np.stack(outs, 0)
